# version 04
# baseline (speedup 1.0000x reference)
import math
from contextlib import ExitStack

import numpy as np
import ml_dtypes
import concourse.bass as bass
import concourse.mybir as mybir
from concourse.ap import AP
from concourse.bass_core_rust import IndirectOffsetOnAxis
from concourse.bass_utils import run_bass_kernel_spmd

F32 = mybir.dt.float32
BF16 = mybir.dt.bfloat16
I32 = mybir.dt.int32
AF = mybir.ActivationFunctionType
ALU = mybir.AluOpType
AX = mybir.AxisListType

S = 8192
D = 2048
DH = 128
NHC = 4
E = 8
IFF = 1408
TOK = 2048
NKT = D // 128
NCI = IFF // 128
CAP = 384
NSL = 2 * E * CAP
RSQ = 1.0 / math.sqrt(128.0)

ENGINES = ("tensor", "vector", "scalar", "gpsimd", "sync")


class Sem:
    __slots__ = ("h", "n")


class Plan:
    def __init__(self, nc, es):
        self.nc = nc
        self.es = es
        self.streams = {k: [] for k in ENGINES}

    def sem(self, name):
        s = Sem()
        s.h = self.es.enter_context(self.nc.semaphore(name))
        s.n = 0
        return s

    def op(self, eng, fn, inc=None, k=1):
        if inc is not None:
            inc.n += k

            def run(EE, fn=fn, h=inc.h, k=k):
                fn(EE).then_inc(h, k)
        else:
            def run(EE, fn=fn):
                fn(EE)
        self.streams[eng].append(run)
        return inc.n if inc is not None else None

    def wait(self, eng, sem, v=None):
        v = sem.n if v is None else v
        if v <= 0:
            return
        self.streams[eng].append(lambda EE, h=sem.h, v=v: EE.wait_ge(h, v))


def build(dbg=False):
    nc = bass.Bass("TRN2", target_bir_lowering=False)

    xT = nc.dram_tensor("xT", [D, S], F32, kind="ExternalInput").ap()
    x_own = nc.dram_tensor("x_own", [TOK, D], F32, kind="ExternalInput").ap()
    wqT = nc.dram_tensor("wqT", [D, NHC * DH], F32, kind="ExternalInput").ap()
    wkT = nc.dram_tensor("wkT", [D, NHC * DH], F32, kind="ExternalInput").ap()
    wvT = nc.dram_tensor("wvT", [D, NHC * DH], F32, kind="ExternalInput").ap()
    woT = nc.dram_tensor("woT", [D, D], F32, kind="ExternalInput").ap()
    routerT = nc.dram_tensor("routerT", [D, E], F32, kind="ExternalInput").ap()
    w1T = nc.dram_tensor("w1T", [E, D, 2 * IFF], F32, kind="ExternalInput").ap()
    w2T = nc.dram_tensor("w2T", [E, IFF, D], F32, kind="ExternalInput").ap()
    cosT = nc.dram_tensor("cosT", [S, 64], F32, kind="ExternalInput").ap()
    sinT = nc.dram_tensor("sinT", [S, 64], F32, kind="ExternalInput").ap()
    identD = nc.dram_tensor("identD", [128, 128], F32, kind="ExternalInput").ap()
    diagD = nc.dram_tensor("diagD", [128, 128], F32, kind="ExternalInput").ap()
    agidx = nc.dram_tensor("agidx", [128, NKT], I32, kind="ExternalInput").ap()
    ltcD = nc.dram_tensor("ltcD", [128, 128], F32, kind="ExternalInput").ap()
    sbaseD = nc.dram_tensor("sbaseD", [1, 128], F32, kind="ExternalInput").ap()

    out = nc.dram_tensor("out", [TOK, D], F32, kind="ExternalOutput").ap()

    attn_self = nc.dram_tensor("attn_self", [2 * NHC * DH, S // 2], BF16)
    attn_gat = nc.dram_tensor("attn_gat", [2 * 16 * DH, S // 2], BF16)
    h_tok_dram = nc.dram_tensor("h_tok_dram", [TOK, D], BF16)
    xs_dram = nc.dram_tensor("xs_dram", [2 * E * 384, D], BF16)
    w1b_dram = nc.dram_tensor("w1b_dram", [E, D, 2 * IFF], BF16)
    w2b_dram = nc.dram_tensor("w2b_dram", [E, IFF, D], BF16)
    ys_dram = nc.dram_tensor("ys_dram", [2 * E * 384, D], BF16)
    groups = [[0, 1, 2, 3], [4, 5, 6, 7]]

    if dbg:
        dbg_attn = nc.dram_tensor("dbg_attn", [16 * DH, S], BF16, kind="ExternalOutput").ap()
        dbg_h = nc.dram_tensor("dbg_h", [TOK, D], BF16, kind="ExternalOutput").ap()
        dbg_g = nc.dram_tensor("dbg_g", [128, 16, E], F32, kind="ExternalOutput").ap()
        dbg_den = nc.dram_tensor("dbg_den", [32, 1024], F32, kind="ExternalOutput").ap()
        dbg_s12 = nc.dram_tensor("dbg_s12", [128, 16, 2], I32, kind="ExternalOutput").ap()
        dbg_slot = nc.dram_tensor("dbg_slot", [128, 16, E], I32, kind="ExternalOutput").ap()
        dbg_xs = nc.dram_tensor("dbg_xs", [NSL, D], BF16, kind="ExternalOutput").ap()
        dbg_ys = nc.dram_tensor("dbg_ys", [NSL, D], BF16, kind="ExternalOutput").ap()
        dbg_yacc = nc.dram_tensor("dbg_yacc", [128, 16, D], BF16, kind="ExternalOutput").ap()
        dbg_xsT = nc.dram_tensor("dbg_xsT", [128, NKT, 768], BF16, kind="ExternalOutput").ap()
        dbg_ya2 = nc.dram_tensor("dbg_ya2", [128, NCI, 768], BF16, kind="ExternalOutput").ap()
        dbg_qT = nc.dram_tensor("dbg_qT", [NHC * 128, 1024], BF16, kind="ExternalOutput").ap()
        dbg_y = nc.dram_tensor("dbg_y", [128, 8, D], BF16, kind="ExternalOutput").ap()
        dbg_ya = nc.dram_tensor("dbg_ya", [128, NCI, 1024], BF16, kind="ExternalOutput").ap()
        dbg_pt = nc.dram_tensor("dbg_pt", [128, 1024], BF16, kind="ExternalOutput").ap()

    es = ExitStack()
    with es:
        p = Plan(nc, es)
        gld = p.sem("gld")
        sld = p.sem("sld")
        pe = p.sem("pe")
        act = p.sem("act")
        dve = p.sem("dve")
        st = p.sem("st")
        cc = p.sem("cc")

        sb = lambda name, shape, dt: es.enter_context(nc.sbuf_tensor(name, shape, dt))

        ident_sb = sb("ident_sb", [128, 128], BF16)
        diag_sb = sb("diag_sb", [128, 128], BF16)
        ones_col = sb("ones_col", [128, 1], BF16)
        ones_row = sb("ones_row", [1, 128], F32)
        ones_rowb = sb("ones_rowb", [1, 128], BF16)
        eps_sb = sb("eps_sb", [128, 1], F32)
        routerT_sb = sb("routerT_sb", [128, NKT, E], BF16)
        g_sb = sb("g_sb", [128, 16, E], F32)
        agidx_sb = sb("agidx_sb", [128, NKT], I32)
        ltc_sb = sb("ltc_sb", [128, 128], BF16)
        sbase_sb = sb("sbase_sb", [1, 128], F32)
        slot_sb = sb("slot_sb", [128, 16, E], I32)
        slot12_sb = sb("slot12_sb", [128, 16, 2], I32)

        p.op("gpsimd", lambda g: g.dma_start(out=ident_sb[:], in_=identD), gld, 16)
        p.op("gpsimd", lambda g: g.dma_start(out=diag_sb[:], in_=diagD), gld, 16)
        p.op("gpsimd", lambda g: g.dma_start(
            out=routerT_sb[:], in_=routerT.rearrange("(k p) e -> p k e", p=128)), gld, 16)
        p.op("gpsimd", lambda g: g.memset(ones_col[:], 1.0))
        p.op("gpsimd", lambda g: g.memset(ones_row[:], 1.0))
        p.op("gpsimd", lambda g: g.memset(ones_rowb[:], 1.0))
        p.op("gpsimd", lambda g: g.memset(eps_sb[:], 1e-5))
        p.op("sync", lambda sy: sy.dma_start(out=agidx_sb[:], in_=agidx), sld, 16)
        p.op("gpsimd", lambda g: g.dma_start(out=ltc_sb[:], in_=ltcD), gld, 16)
        p.op("sync", lambda sy: sy.dma_start(out=sbase_sb[:], in_=sbaseD), sld, 16)
        consts_gld = gld.n

        esAB = ExitStack()
        kT_sb = esAB.enter_context(nc.sbuf_tensor("kT_sb", [128, NHC, S], BF16))
        v_sb = esAB.enter_context(nc.sbuf_tensor("v_sb", [128, S // 128, NHC * DH], BF16))

        with ExitStack() as esA:
            sbA = lambda n_, sh, dt: esA.enter_context(nc.sbuf_tensor(n_, sh, dt))
            wk_sb = sbA("wk_sb", [128, NKT, NHC * DH], BF16)
            wv_sb = sbA("wv_sb", [128, NKT, NHC * DH], BF16)
            xTc = [sbA(f"xTc{i}", [128, NKT, 512], BF16) for i in range(2)]
            cos_c = [sbA(f"cos_c{i}", [128, 4, 64], F32) for i in range(2)]
            sin_c = [sbA(f"sin_c{i}", [128, 4, 64], F32) for i in range(2)]
            t_scr = [sbA(f"t_scr{i}", [128, 4, 64], F32) for i in range(4)]
            ktmp = sbA("ktmp", [128, 512], BF16)
            psK = esA.enter_context(nc.psum_tensor("psK", [128, 512], F32))
            psV = esA.enter_context(nc.psum_tensor("psV", [128, 512], F32))
            psT = esA.enter_context(nc.psum_tensor("psT", [128, 512], BF16))

            p.op("gpsimd", lambda g: g.dma_start(
                out=wk_sb[:], in_=wkT.rearrange("(k p) m -> p k m", p=128)), gld, 16)
            p.op("gpsimd", lambda g: g.dma_start(
                out=wv_sb[:], in_=wvT.rearrange("(k p) m -> p k m", p=128)), gld, 16)
            wA_gld = gld.n

            cast_jobs = []
            for e_ in range(E):
                for j_ in range(4):
                    cast_jobs.append((w1b_dram.ap()[e_, j_ * 512:(j_ + 1) * 512, :],
                                      w1T[e_, j_ * 512:(j_ + 1) * 512, :]))
            for e_ in range(E):
                for j_ in range(2):
                    cast_jobs.append((w2b_dram.ap()[e_, j_ * 704:(j_ + 1) * 704, :],
                                      w2T[e_, j_ * 704:(j_ + 1) * 704, :]))
            xTc_user = [0, 0]
            cos_user = [0, 0]
            xTc_ld = [0, 0]
            cos_ld = [0, 0]
            psK_rd = 0
            psT_rd = 0
            psV_rd = 0
            ktmp_rd = 0

            for cch in range(16):
                bf = cch % 2
                p.wait("gpsimd", pe, xTc_user[bf])
                xTc_ld[bf] = p.op("gpsimd", lambda g, c=cch, b=bf: g.dma_start(
                    out=xTc[b][:], in_=xT[:, c * 512:(c + 1) * 512]
                    .rearrange("(k p) s -> p k s", p=128)), gld, 16)

                p.wait("sync", dve, cos_user[bf])
                p.op("sync", lambda sy, c=cch, b=bf: sy.dma_start(
                    out=cos_c[b][:], in_=cosT[c * 512:(c + 1) * 512, :]
                    .rearrange("(t p) f -> p t f", p=128)), sld, 16)
                cos_ld[bf] = p.op("sync", lambda sy, c=cch, b=bf: sy.dma_start(
                    out=sin_c[b][:], in_=sinT[c * 512:(c + 1) * 512, :]
                    .rearrange("(t p) f -> p t f", p=128)), sld, 16)

                for stl in range(4):
                    g_tile = cch * 4 + stl
                    p.wait("tensor", gld, max(xTc_ld[bf], wA_gld))
                    p.wait("tensor", dve, psK_rd)
                    for kt in range(NKT):
                        p.op("tensor", lambda t, b=bf, kt=kt, s=stl: t.matmul(
                            psK[:], xTc[b][:, kt, s * 128:(s + 1) * 128],
                            wk_sb[:, kt, :], start=(kt == 0), stop=(kt == NKT - 1)),
                            pe if kt == NKT - 1 else None)
                    peK = pe.n
                    p.wait("tensor", act, psV_rd)
                    for kt in range(NKT):
                        p.op("tensor", lambda t, b=bf, kt=kt, s=stl: t.matmul(
                            psV[:], xTc[b][:, kt, s * 128:(s + 1) * 128],
                            wv_sb[:, kt, :], start=(kt == 0), stop=(kt == NKT - 1)),
                            pe if kt == NKT - 1 else None)
                    peV = pe.n
                    p.wait("vector", pe, peK)
                    p.wait("vector", sld, cos_ld[bf])
                    p.wait("vector", pe, ktmp_rd)
                    ps3 = psK[:].rearrange("p (h x) -> p h x", h=4)
                    x1, x2 = ps3[:, :, 0:64], ps3[:, :, 64:128]
                    cb = AP(cos_c[bf], stl * 64, [[256, 128], [0, 4], [1, 64]])
                    sbn = AP(sin_c[bf], stl * 64, [[256, 128], [0, 4], [1, 64]])
                    p.op("vector", lambda v, x1=x1, cb=cb: v.tensor_mul(t_scr[0][:], x1, cb))
                    p.op("vector", lambda v, x2=x2, sbn=sbn: v.tensor_mul(t_scr[1][:], x2, sbn))
                    p.op("vector", lambda v, x1=x1, sbn=sbn: v.tensor_mul(t_scr[2][:], x1, sbn))
                    p.op("vector", lambda v, x2=x2, cb=cb: v.tensor_mul(t_scr[3][:], x2, cb), dve)
                    psK_rd = dve.n
                    p.wait("vector", dve, psK_rd)
                    kt3 = ktmp[:].rearrange("p (h x) -> p h x", h=4)
                    p.op("vector", lambda v, kt3=kt3: v.tensor_sub(
                        kt3[:, :, 0:64], t_scr[0][:], t_scr[1][:]))
                    p.op("vector", lambda v, kt3=kt3: v.tensor_add(
                        kt3[:, :, 64:128], t_scr[2][:], t_scr[3][:]), dve)
                    kt_ready = dve.n
                    p.op("scalar", lambda s_, v=peV: s_.wait_ge(pe.h, v))
                    p.op("scalar", lambda s_, g=g_tile: s_.copy(v_sb[:, g, :], psV[:]), act)
                    psV_rd = act.n
                    p.wait("tensor", dve, max(kt_ready, psT_rd))
                    for h in range(NHC):
                        p.op("tensor", lambda t, h=h: t.transpose(
                            psT[:, h * 128:(h + 1) * 128],
                            ktmp[:, h * 128:(h + 1) * 128], ident_sb[:]),
                            pe if h == NHC - 1 else None)
                    ktmp_rd = pe.n
                    peT = pe.n
                    p.wait("vector", pe, peT)
                    dstK = kT_sb[:, :, g_tile * 128:(g_tile + 1) * 128]
                    srcT = psT[:].rearrange("p (h x) -> p h x", h=4)
                    p.op("vector", lambda v, d=dstK, s_=srcT: v.tensor_copy(d, s_), dve)
                    psT_rd = dve.n
                xTc_user[bf] = pe.n
                cos_user[bf] = dve.n
            phaseA_pe = pe.n

        with ExitStack() as esB:
            sbB = lambda n_, sh, dt: esB.enter_context(nc.sbuf_tensor(n_, sh, dt))
            wq_sb = sbB("wq_sb", [128, NKT, NHC * DH], BF16)
            xTI = sbB("xTI", [128, NKT, 1024], BF16)
            cos_I = sbB("cos_I", [128, 8, 64], F32)
            sin_I = sbB("sin_I", [128, 8, 64], F32)
            tB = [sbB(f"tB{i}", [128, 4, 64], F32) for i in range(2)]
            qtmp = sbB("qtmp", [128, 512], BF16)
            qT_sb = sbB("qT_sb", [128, NHC, 1024], BF16)
            pt = [sbB(f"pt{i}", [128, 1024], BF16) for i in range(2)]
            rb_sb = sbB("rb_sb", [128, 1024], F32)
            rdb_sb = sbB("rdb_sb", [1, 1024], BF16)
            o_sb = sbB("o_sb", [128, 1024], BF16)

            p.wait("gpsimd", pe, phaseA_pe)
            p.op("gpsimd", lambda g: g.dma_start(
                out=wq_sb[:], in_=wqT.rearrange("(k p) m -> p k m", p=128)), gld, 16)
            wq_ld = gld.n
            p.wait("sync", pe, phaseA_pe)

            xTI_user = 0
            cosI_user = 0
            o_store = [0]
            attn_half_st = 0
            for I in range(8):
                p.wait("gpsimd", pe, xTI_user)
                xTI_ld = p.op("gpsimd", lambda g, I=I: g.dma_start(
                    out=xTI[:], in_=xT[:, I * 1024:(I + 1) * 1024]
                    .rearrange("(k p) s -> p k s", p=128)), gld, 16)
                p.wait("sync", dve, cosI_user)
                p.op("sync", lambda sy, I=I: sy.dma_start(
                    out=cos_I[:], in_=cosT[I * 1024:(I + 1) * 1024, :]
                    .rearrange("(t p) f -> p t f", p=128)), sld, 16)
                cosI_ld = p.op("sync", lambda sy, I=I: sy.dma_start(
                    out=sin_I[:], in_=sinT[I * 1024:(I + 1) * 1024, :]
                    .rearrange("(t p) f -> p t f", p=128)), sld, 16)

                superA_act, superA_dve = act.n, dve.n

                with ExitStack() as esQ:
                    psQ = esQ.enter_context(nc.psum_tensor(f"psQ{I}", [128, 512], F32))
                    psTq = esQ.enter_context(nc.psum_tensor(f"psTq{I}", [128, 512], BF16))
                    psQ_rd = 0
                    psTq_rd = 0
                    qtmp_rd = 0
                    first_q = True
                    for stl in range(8):
                        p.wait("tensor", gld, max(xTI_ld, wq_ld))
                        if first_q:
                            p.wait("tensor", act, superA_act)
                            p.wait("tensor", dve, superA_dve)
                            first_q = False
                        p.wait("tensor", dve, psQ_rd)
                        for kt in range(NKT):
                            p.op("tensor", lambda t, kt=kt, s=stl: t.matmul(
                                psQ[:], xTI[:, kt, s * 128:(s + 1) * 128],
                                wq_sb[:, kt, :], start=(kt == 0), stop=(kt == NKT - 1)),
                                pe if kt == NKT - 1 else None)
                        peQ = pe.n
                        p.wait("vector", pe, max(peQ, qtmp_rd))
                        p.wait("vector", sld, cosI_ld)
                        ps3 = psQ[:].rearrange("p (h x) -> p h x", h=4)
                        x1, x2 = ps3[:, :, 0:64], ps3[:, :, 64:128]
                        cb = AP(cos_I, stl * 64, [[512, 128], [0, 4], [1, 64]])
                        sbn = AP(sin_I, stl * 64, [[512, 128], [0, 4], [1, 64]])
                        p.op("vector", lambda v, x1=x1, cb=cb: v.tensor_mul(tB[0][:], x1, cb))
                        p.op("vector", lambda v, x2=x2, sbn=sbn: v.tensor_mul(tB[1][:], x2, sbn), dve)
                        p.wait("vector", dve, dve.n)
                        qt3 = qtmp[:].rearrange("p (h x) -> p h x", h=4)
                        p.op("vector", lambda v, qt3=qt3: v.tensor_sub(
                            qt3[:, :, 0:64], tB[0][:], tB[1][:]))
                        p.op("vector", lambda v, x1=x1, sbn=sbn: v.tensor_mul(tB[0][:], x1, sbn))
                        p.op("vector", lambda v, x2=x2, cb=cb: v.tensor_mul(tB[1][:], x2, cb), dve)
                        psQ_rd = dve.n
                        p.wait("vector", dve, psQ_rd)
                        p.op("vector", lambda v, qt3=qt3: v.tensor_add(
                            qt3[:, :, 64:128], tB[0][:], tB[1][:]), dve)
                        qt_ready = dve.n
                        p.wait("tensor", dve, max(qt_ready, psTq_rd))
                        for h in range(NHC):
                            p.op("tensor", lambda t, h=h: t.transpose(
                                psTq[:, h * 128:(h + 1) * 128],
                                qtmp[:, h * 128:(h + 1) * 128], ident_sb[:]),
                                pe if h == NHC - 1 else None)
                        qtmp_rd = pe.n
                        peTq = pe.n
                        p.wait("vector", pe, peTq)
                        dstQ = qT_sb[:, :, stl * 128:(stl + 1) * 128]
                        srcTq = psTq[:].rearrange("p (h x) -> p h x", h=4)
                        p.op("vector", lambda v, d=dstQ, s_=srcTq: v.tensor_copy(d, s_), dve)
                        psTq_rd = dve.n
                    qT_ready = dve.n
                if dbg and I == 7:
                    p.wait("sync", dve, qT_ready)
                    qT_st = p.op("sync", lambda sy: sy.dma_start(
                        out=dbg_qT.rearrange("(h p) q -> p h q", p=128), in_=qT_sb[:]), st, 16)
                xTI_user = pe.n
                cosI_user = dve.n

                with ExitStack() as esS:
                    ps_sc = [esS.enter_context(nc.psum_tensor(f"ps_sc{I}_{i}", [128, 1024], F32))
                             for i in range(2)]
                    ps_o = esS.enter_context(nc.psum_tensor(f"ps_o{I}", [128, 1024], F32))
                    ps_d = esS.enter_context(nc.psum_tensor(f"ps_d{I}", [1, 1024], F32))

                    sc_rd = [act.n, act.n]
                    pt_rd = [pe.n, pe.n]
                    o_rd = dve.n
                    d_rd = dve.n
                    bc_rd = dve.n
                    njj = 8 * I + 8

                    def emit_av(info):
                        (j, bf, qs, parts, exp_done, mask_done, first, last) = info
                        qoff = qs * 128
                        p.op("tensor", lambda t, v=exp_done: t.wait_ge(act.h, v))
                        if mask_done is not None:
                            p.op("tensor", lambda t, v=mask_done: t.wait_ge(dve.h, v))
                        for (o_, n_) in parts:
                            p.op("tensor", lambda t, b=bf, o_=o_, n_=n_, q=qoff, j=j, f=first, l=last, h=hh: t.matmul(
                                ps_o[:, q + o_:q + o_ + n_],
                                v_sb[:, j, h * 128:(h + 1) * 128],
                                pt[b][:, o_:o_ + n_], start=f, stop=l))
                        for ip, (o_, n_) in enumerate(parts):
                            p.op("tensor", lambda t, b=bf, o_=o_, n_=n_, q=qoff, f=first, l=last: t.matmul(
                                ps_d[0:1, q + o_:q + o_ + n_], ones_col[:],
                                pt[b][:, o_:o_ + n_], start=f, stop=l),
                                pe if ip == len(parts) - 1 else None)
                        pt_rd[bf] = pe.n

                    for hh in range(NHC):
                        pending = None
                        for j in range(njj):
                            bf = j % 2
                            qs = max(j - 8 * I, 0)
                            N = (8 - qs) * 128
                            qoff = qs * 128
                            parts = [(0, min(N, 512))]
                            if N > 512:
                                parts.append((512, N - 512))
                            p.wait("tensor", dve, qT_ready)
                            p.wait("tensor", act, sc_rd[bf])
                            if j <= 1:
                                p.wait("tensor", dve, max(bc_rd, d_rd, o_rd))
                            for ip, (o_, n_) in enumerate(parts):
                                p.op("tensor", lambda t, h=hh, j=j, b=bf, o_=o_, n_=n_, q=qoff: t.matmul(
                                    ps_sc[b][:, o_:o_ + n_],
                                    kT_sb[:, h, j * 128:(j + 1) * 128],
                                    qT_sb[:, h, q + o_:q + o_ + n_],
                                    start=True, stop=True),
                                    pe if ip == len(parts) - 1 else None)
                            sc_done = pe.n
                            if pending is not None:
                                emit_av(pending)
                            p.op("scalar", lambda s_, v=sc_done: s_.wait_ge(pe.h, v))
                            p.op("scalar", lambda s_, v=pt_rd[bf]: s_.wait_ge(pe.h, v))
                            p.op("scalar", lambda s_, b=bf, N=N: s_.activation(
                                pt[b][:, 0:N], ps_sc[b][:, 0:N], AF.Exp, scale=RSQ), act)
                            sc_rd[bf] = act.n
                            exp_done = act.n
                            mask_done = None
                            if dbg and I == 7 and hh == 0 and j == 0:
                                p.wait("sync", act, exp_done)
                                pt_st = p.op("sync", lambda sy, b=bf: sy.dma_start(
                                    out=dbg_pt, in_=pt[b][:]), st, 16)
                                p.wait("scalar", st, pt_st)
                            if j >= 8 * I:
                                p.wait("vector", act, exp_done)
                                p.op("vector", lambda v, b=bf: v.tensor_mul(
                                    pt[b][:, 0:128], pt[b][:, 0:128], diag_sb[:]), dve)
                                mask_done = dve.n
                            pending = (j, bf, qs, parts, exp_done, mask_done,
                                       j == 0, j == njj - 1)
                        emit_av(pending)
                        av_all = pe.n
                        p.wait("vector", pe, av_all)
                        def _recip(v):
                            with nc.allow_low_precision(reason="softmax 1/denom broadcast in bf16 (tolerance budgeted)"):
                                return v.reciprocal(rdb_sb[0:1, :], ps_d[0:1, :])
                        p.op("vector", _recip, dve)
                        d_rd = dve.n
                        if dbg:
                            p.wait("sync", dve, d_rd)
                            den_st = p.op("sync", lambda sy, I=I, h=hh: sy.dma_start(
                                out=dbg_den[I * 4 + h:I * 4 + h + 1, :],
                                in_=rb_sb[0:1, :]), st, 16)
                            p.wait("vector", st, den_st)
                        p.wait("tensor", dve, d_rd)
                        p.wait("tensor", act, sc_rd[1])
                        for ip, o_ in enumerate((0, 512)):
                            p.op("tensor", lambda t, o_=o_: t.matmul(
                                ps_sc[1][:, o_:o_ + 512], ones_rowb[:],
                                rdb_sb[0:1, o_:o_ + 512], start=True, stop=True),
                                pe if ip == 1 else None)
                        bc_done = pe.n
                        p.wait("vector", pe, bc_done)
                        p.op("vector", lambda v: v.tensor_copy(rb_sb[:], ps_sc[1][:]), dve)
                        p.wait("vector", dve, dve.n)
                        p.wait("vector", st, o_store[0])
                        p.op("vector", lambda v: v.tensor_mul(o_sb[:], ps_o[:], rb_sb[:]), dve)
                        o_rd = dve.n
                        bc_rd = o_rd
                        sc_rd[1] = act.n
                        p.wait("scalar", dve, o_rd)
                        p.op("scalar", lambda s_, h=hh, I=I: s_.dma_start(
                            out=attn_self.ap()[(I // 4) * 512 + h * 128:(I // 4) * 512 + (h + 1) * 128,
                                               (I % 4) * 1024:(I % 4 + 1) * 1024],
                            in_=o_sb[:]), st, 16)
                        o_store[0] = st.n
                if I == 3:
                    attn_half_st = st.n
                    p.wait("gpsimd", st, attn_half_st)
                    p.op("gpsimd", lambda g: g.collective_compute(
                        "AllGather", ALU.bypass, replica_groups=groups,
                        ins=[attn_self.ap()[0:512, :].opt()],
                        outs=[attn_gat.ap()[0:2048, :].opt()]), cc, 1)
            attn_done_st = st.n
        esAB.close()

        for dst_, src_ in cast_jobs:
            p.op("gpsimd", lambda g, d_=dst_, s2_=src_: g.dma_start(out=d_, in_=s2_), gld, 16)
        wcast_done = gld.n

        p.wait("gpsimd", st, attn_done_st)
        p.op("gpsimd", lambda g: g.collective_compute(
            "AllGather", ALU.bypass, replica_groups=groups,
            ins=[attn_self.ap()[512:1024, :].opt()],
            outs=[attn_gat.ap()[2048:4096, :].opt()]), cc, 1)
        cc_done = cc.n

        with ExitStack() as esC:
            sbC = lambda n_, sh, dt: esC.enter_context(nc.sbuf_tensor(n_, sh, dt))
            htok_all = sbC("htok_all", [128, 16, D], BF16)
            wo_sb = sbC("wo_sb", [128, NKT, D], BF16)
            ag_sb = sbC("ag_sb", [128, NKT, 512], BF16)
            xo_sb = [sbC(f"xo_sb{i}", [128, D], F32) for i in range(2)]
            hp_sb = sbC("hp_sb", [128, D], F32)
            sq_scr = sbC("sq_scr", [128, D], BF16)
            hTt_sb = sbC("hTt_sb", [128, NKT, 128], BF16)
            sum1 = sbC("sum1", [128, 1], F32)
            ssq = sbC("ssq", [128, 1], F32)
            negmean = sbC("negmean", [128, 1], F32)
            m2t = sbC("m2t", [128, 1], F32)
            var = sbC("var", [128, 1], F32)
            srt = sbC("srt", [128, 1], F32)
            rstd = sbC("rstd", [128, 1], F32)
            lmax = sbC("lmax", [128, 1], F32)
            l2 = sbC("l2", [128, E], F32)
            lmax2 = sbC("lmax2", [128, 1], F32)
            msk1 = sbC("msk1", [128, E], F32)
            msk2 = sbC("msk2", [128, E], F32)
            w1g = sbC("w1g", [128, 1], F32)
            w2g = sbC("w2g", [128, 1], F32)
            selb_sb = sbC("selb_sb", [128, 16, E], BF16)
            pfx_sb = sbC("pfx_sb", [128, 16, E], F32)
            tot_sb = sbC("tot_sb", [1, 128], F32)
            toff_sb = sbC("toff_sb", [1, 128], F32)
            btoff_sb = sbC("btoff_sb", [128, 16, E], F32)
            slotf_sb = sbC("slotf_sb", [128, E], F32)
            inv_sb = sbC("inv_sb", [128, E], F32)
            ovr_sb = sbC("ovr_sb", [128, E], F32)
            m1_all = sbC("m1_all", [128, 16, E], BF16)
            slotr_sb = sbC("slotr_sb", [128, E], F32)
            msk2t = sbC("msk2t", [128, E], F32)
            sl_sb = sbC("sl_sb", [128, 1], F32)
            ps_wo = esC.enter_context(nc.psum_tensor("ps_wo", [128, D], F32))
            psTc = esC.enter_context(nc.psum_tensor("psTc", [128, 512], BF16))
            ps_r = esC.enter_context(nc.psum_tensor("ps_r", [128, E], F32))
            ps_p = esC.enter_context(nc.psum_tensor("ps_p", [128, E], F32))
            ps_t1 = esC.enter_context(nc.psum_tensor("ps_t1", [1, E], F32))

            for i4 in range(4):
                p.op("gpsimd", lambda g, i4=i4: g.dma_start(
                    out=wo_sb[:, i4 * 4:(i4 + 1) * 4, :],
                    in_=woT[i4 * 512:(i4 + 1) * 512, :].rearrange("(k p) n -> p k n", p=128)),
                    gld, 16)
            wo_ld = gld.n

            entryC_act, entryC_dve = act.n, dve.n
            bregs = {}
            p.op("gpsimd", lambda g: bregs.__setitem__("scat", g.to_reg(NSL - 1)))
            p.op("gpsimd", lambda g: bregs.__setitem__("gat", g.to_reg(NSL * D - 1)))
            ag_ld = 0
            ag_user = 0
            hp_rd = 0
            wo_add = 0
            psTc_rd = 0
            htok_rd_pe = 0
            gate_done = 0
            xo_ld = [0, 0]
            xo_user = [0, 0]
            firstC = True
            ps_p_rd = 0
            ps_t1_rd = 0
            h_st = 0
            scat_done = [0, 0]
            slotC_done = 0

            for c4 in range(4):
                p.wait("gpsimd", cc, cc_done)
                p.wait("gpsimd", pe, ag_user)
                for kt in range(NKT):
                    p.op("gpsimd", lambda g, kt=kt, c4=c4: g.indirect_dma_start(
                        ag_sb[:, kt, :], None,
                        attn_gat.ap()[0:128, 0:512],
                        IndirectOffsetOnAxis(agidx_sb[:, kt:kt + 1], 1),
                        element_offset=c4 * 512), gld, 16)
                ag_ld = gld.n
                for tsub in range(4):
                    tt = c4 * 4 + tsub
                    xb = tt % 2
                    p.wait("sync", cc, cc_done)
                    p.wait("sync", dve, xo_user[xb])
                    xo_ld[xb] = p.op("sync", lambda sy, tt=tt, b=xb: sy.dma_start(
                        out=xo_sb[b][:], in_=x_own[tt * 128:(tt + 1) * 128, :]), sld, 16)
                    p.wait("tensor", gld, max(ag_ld, wo_ld))
                    if firstC:
                        p.wait("tensor", act, entryC_act)
                        p.wait("tensor", dve, entryC_dve)
                        firstC = False
                    p.wait("tensor", dve, wo_add)
                    for n4 in range(4):
                        for kt in range(NKT):
                            p.op("tensor", lambda t, n4=n4, kt=kt, ts=tsub: t.matmul(
                                ps_wo[:, n4 * 512:(n4 + 1) * 512],
                                ag_sb[:, kt, ts * 128:(ts + 1) * 128],
                                wo_sb[:, kt, n4 * 512:(n4 + 1) * 512],
                                start=(kt == 0), stop=(kt == NKT - 1)),
                                pe if (n4 == 3 and kt == NKT - 1) else None)
                    wo_done = pe.n
                    p.wait("vector", pe, wo_done)
                    p.wait("vector", sld, xo_ld[xb])
                    p.wait("vector", act, hp_rd)
                    p.op("vector", lambda v, b=xb: v.tensor_add(hp_sb[:], xo_sb[b][:], ps_wo[:]), dve)
                    a1 = dve.n
                    wo_add = a1
                    xo_user[xb] = a1
                    p.op("scalar", lambda s_, v=a1: s_.wait_ge(dve.h, v))
                    p.op("scalar", lambda s_: s_.activation(
                        sq_scr[:], hp_sb[:], AF.Square, accum_out=ssq[:]), act)
                    hp_rd = act.n
                    sq_done = act.n
                    p.wait("vector", dve, a1)
                    p.op("vector", lambda v: v.reduce_sum(sum1[:], hp_sb[:], axis=AX.X), dve)
                    a2 = dve.n
                    p.wait("vector", dve, a2)
                    p.op("vector", lambda v: v.tensor_scalar(
                        negmean[:], sum1[:], -1.0 / D, None, ALU.mult), dve)
                    a3 = dve.n
                    p.wait("vector", dve, a3)
                    p.op("vector", lambda v: v.tensor_mul(m2t[:], negmean[:], negmean[:]))
                    p.wait("vector", act, sq_done)
                    p.op("vector", lambda v: v.tensor_scalar(
                        var[:], ssq[:], 1.0 / D, None, ALU.mult), dve)
                    a4 = dve.n
                    p.wait("vector", dve, a4)
                    p.op("vector", lambda v: v.tensor_sub(var[:], var[:], m2t[:]), dve)
                    a5 = dve.n
                    p.op("scalar", lambda s_, v=a5: s_.wait_ge(dve.h, v))
                    p.op("scalar", lambda s_: s_.activation(
                        srt[:], var[:], AF.Sqrt, bias=eps_sb[:]), act)
                    a6 = act.n
                    p.wait("vector", act, a6)
                    p.op("vector", lambda v: v.reciprocal(rstd[:], srt[:]), dve)
                    a7 = dve.n
                    p.wait("vector", dve, a7)
                    p.op("vector", lambda v, tt=tt: v.tensor_scalar(
                        htok_all[:, tt, :], hp_sb[:], negmean[:], rstd[:],
                        ALU.add, ALU.mult), dve)
                    htok_done = dve.n
                    p.wait("scalar", dve, htok_done)
                    h_st = p.op("scalar", lambda s_, tt=tt: s_.dma_start(
                        out=h_tok_dram.ap()[tt * 128:(tt + 1) * 128, :],
                        in_=htok_all[:, tt, :]), st, 16)
                    for ktg in range(4):
                        p.wait("tensor", dve, max(htok_done, psTc_rd))
                        p.wait("tensor", pe, htok_rd_pe)
                        for i in range(4):
                            p.op("tensor", lambda t, ktg=ktg, i=i, tt=tt: t.transpose(
                                psTc[:, i * 128:(i + 1) * 128],
                                htok_all[:, tt, (ktg * 4 + i) * 128:(ktg * 4 + i + 1) * 128],
                                ident_sb[:]), pe if i == 3 else None)
                        trd = pe.n
                        p.wait("vector", pe, trd)
                        dstH = hTt_sb[:, ktg * 4:(ktg + 1) * 4, :]
                        srcH = psTc[:].rearrange("p (k x) -> p k x", k=4)
                        p.op("vector", lambda v, d=dstH, s_=srcH: v.tensor_copy(d, s_), dve)
                        psTc_rd = dve.n
                    hT_done = dve.n
                    p.wait("tensor", dve, max(hT_done, gate_done))
                    for kt in range(NKT):
                        p.op("tensor", lambda t, kt=kt: t.matmul(
                            ps_r[:], hTt_sb[:, kt, :],
                            routerT_sb[:, kt, :], start=(kt == 0), stop=(kt == NKT - 1)),
                            pe if kt == NKT - 1 else None)
                    r_done = pe.n
                    htok_rd_pe = 0
                    p.wait("vector", pe, r_done)
                    p.op("vector", lambda v: v.reduce_max(lmax[:], ps_r[:], axis=AX.X), dve)
                    b1 = dve.n
                    p.wait("vector", dve, b1)
                    p.op("vector", lambda v: v.tensor_scalar(
                        msk1[:], ps_r[:], lmax[:], None, ALU.is_ge), dve)
                    b2 = dve.n
                    p.wait("vector", dve, b2)
                    p.op("vector", lambda v, tt=tt: v.tensor_copy(m1_all[:, tt, :], msk1[:]))
                    p.op("vector", lambda v: v.scalar_tensor_tensor(
                        l2[:], msk1[:], -1e30, ps_r[:], ALU.mult, ALU.add), dve)
                    b3 = dve.n
                    p.wait("vector", dve, b3)
                    p.op("vector", lambda v: v.reduce_max(lmax2[:], l2[:], axis=AX.X), dve)
                    b4 = dve.n
                    p.wait("vector", dve, b4)
                    p.op("vector", lambda v: v.tensor_scalar(
                        msk2[:], l2[:], lmax2[:], None, ALU.is_ge), dve)
                    b5 = dve.n
                    p.op("scalar", lambda s_, v=b5: s_.wait_ge(dve.h, v))
                    p.op("scalar", lambda s_: s_.activation(
                        w1g[:], lmax2[:], AF.Sigmoid, bias=lmax[:], scale=-1.0), act)
                    b6 = act.n
                    p.wait("vector", act, b6)
                    p.op("vector", lambda v, tt=tt: v.tensor_scalar(
                        g_sb[:, tt, :], msk1[:], w1g[:], None, ALU.mult))
                    p.op("vector", lambda v: v.tensor_scalar(
                        w2g[:], w1g[:], -1.0, 1.0, ALU.mult, ALU.add), dve)
                    b7 = dve.n
                    p.wait("vector", dve, b7)
                    p.op("vector", lambda v, tt=tt: v.scalar_tensor_tensor(
                        g_sb[:, tt, :], msk2[:], w2g[:], g_sb[:, tt, :],
                        ALU.mult, ALU.add))
                    p.op("vector", lambda v, tt=tt: v.tensor_add(
                        selb_sb[:, tt, :], msk1[:], msk2[:]), dve)
                    gate_done = dve.n
                    p.wait("tensor", dve, max(gate_done, ps_p_rd, ps_t1_rd))
                    p.op("tensor", lambda t, tt=tt: t.matmul(
                        ps_p[:], ltc_sb[:], selb_sb[:, tt, :], start=True, stop=True))
                    p.op("tensor", lambda t, tt=tt: t.matmul(
                        ps_t1[:], ones_col[:], selb_sb[:, tt, :], start=True, stop=True),
                        pe)
                    pfx_done = pe.n
                    p.wait("vector", pe, pfx_done)
                    p.op("vector", lambda v, tt=tt: v.tensor_copy(pfx_sb[:, tt, :], ps_p[:]))
                    p.op("vector", lambda v, tt=tt: v.tensor_copy(
                        tot_sb[0:1, tt * 8:(tt + 1) * 8], ps_t1[0:1, :]), dve)
                    ps_p_rd = dve.n
                    ps_t1_rd = dve.n

                    if tt in (7, 15):
                        hh2 = tt // 8
                        base_col = hh2 * 64
                        p.wait("vector", dve, dve.n)
                        p.op("vector", lambda v, bc=base_col: v.tensor_scalar(
                            toff_sb[0:1, bc:bc + 8], tot_sb[0:1, bc:bc + 8],
                            0.0, None, ALU.mult), dve)
                        for k in range(1, 8):
                            p.wait("vector", dve, dve.n)
                            p.op("vector", lambda v, bc=base_col, k=k: v.tensor_add(
                                toff_sb[0:1, bc + k * 8:bc + (k + 1) * 8],
                                toff_sb[0:1, bc + (k - 1) * 8:bc + k * 8],
                                tot_sb[0:1, bc + (k - 1) * 8:bc + k * 8]), dve)
                        p.wait("vector", dve, dve.n)
                        p.op("vector", lambda v, bc=base_col: v.tensor_add(
                            toff_sb[0:1, bc:bc + 64],
                            toff_sb[0:1, bc:bc + 64],
                            sbase_sb[0:1, bc:bc + 64]), dve)
                        toff_done = dve.n
                        p.wait("tensor", dve, max(toff_done, wo_add))
                        p.op("tensor", lambda t, bc=base_col: t.matmul(
                            ps_wo[:, 0:64], ones_row[:],
                            toff_sb[0:1, bc:bc + 64], start=True, stop=True), pe)
                        bc_done2 = pe.n
                        p.wait("vector", pe, bc_done2)
                        p.op("vector", lambda v, h2=hh2: v.tensor_copy(
                            btoff_sb[:, h2 * 8:(h2 + 1) * 8, :],
                            ps_wo[:, 0:64].rearrange("p (a b) -> p a b", b=8)), dve)
                        btoff_done = dve.n
                        wo_add = dve.n
                        for t2 in range(hh2 * 8, hh2 * 8 + 8):
                            p.wait("vector", dve, dve.n)
                            p.op("vector", lambda v, t2=t2: v.tensor_add(
                                slotf_sb[:], pfx_sb[:, t2, :], btoff_sb[:, t2, :]))
                            p.op("vector", lambda v, t2=t2: v.tensor_scalar(
                                ovr_sb[:], pfx_sb[:, t2, :], CAP + 0.5, None, ALU.is_gt), dve)
                            p.wait("vector", dve, dve.n)
                            p.op("vector", lambda v: v.tensor_scalar(
                                slotf_sb[:], slotf_sb[:], -1.0, None, ALU.add), dve)
                            p.wait("vector", dve, dve.n)
                            p.op("vector", lambda v: v.scalar_tensor_tensor(
                                slotr_sb[:], ovr_sb[:], 1e6, slotf_sb[:], ALU.mult, ALU.add), dve)
                            p.wait("vector", dve, dve.n)
                            p.op("vector", lambda v, t2=t2: v.tensor_scalar(
                                inv_sb[:], selb_sb[:, t2, :], -1.0, 1.0, ALU.mult, ALU.add), dve)
                            p.wait("vector", dve, dve.n)
                            p.op("vector", lambda v: v.scalar_tensor_tensor(
                                slotf_sb[:], inv_sb[:], 1e6, slotr_sb[:], ALU.mult, ALU.add), dve)
                            p.wait("vector", dve, dve.n)
                            p.op("vector", lambda v: v.tensor_scalar(
                                slotf_sb[:], slotf_sb[:], float(D), None, ALU.mult), dve)
                            p.wait("vector", dve, dve.n)
                            p.op("vector", lambda v, t2=t2: v.tensor_copy(
                                slot_sb[:, t2, :], slotf_sb[:]))
                            p.op("vector", lambda v, t2=t2: v.tensor_mul(
                                inv_sb[:], m1_all[:, t2, :], slotr_sb[:]), dve)
                            p.wait("vector", dve, dve.n)
                            p.op("vector", lambda v: v.reduce_sum(sl_sb[:], inv_sb[:], axis=AX.X), dve)
                            p.wait("vector", dve, dve.n)
                            p.op("vector", lambda v, t2=t2: v.tensor_copy(
                                slot12_sb[:, t2, 0:1], sl_sb[:]))
                            p.op("vector", lambda v, t2=t2: v.tensor_sub(
                                msk2t[:], selb_sb[:, t2, :], m1_all[:, t2, :]), dve)
                            p.wait("vector", dve, dve.n)
                            p.op("vector", lambda v: v.tensor_mul(inv_sb[:], msk2t[:], slotr_sb[:]), dve)
                            p.wait("vector", dve, dve.n)
                            p.op("vector", lambda v: v.reduce_sum(sl_sb[:], inv_sb[:], axis=AX.X), dve)
                            p.wait("vector", dve, dve.n)
                            p.op("vector", lambda v, t2=t2: v.tensor_copy(
                                slot12_sb[:, t2, 1:2], sl_sb[:]), dve)
                            slotC_done = dve.n
                        p.wait("gpsimd", dve, slotC_done)
                        for t2 in range(hh2 * 8, hh2 * 8 + 8):
                            for k2 in range(2):
                                p.op("gpsimd", lambda g, t2=t2, k2=k2: g.indirect_dma_start(
                                    xs_dram.ap()[0:128, :],
                                    IndirectOffsetOnAxis(slot12_sb[:, t2, k2:k2 + 1], 0),
                                    htok_all[:, t2, :], None,
                                    bounds_check=bregs["scat"], oob_is_err=False), gld, 16)
                        scat_done[hh2] = gld.n
                ag_user = pe.n
            phaseC_pe = pe.n
            phaseC_act = act.n
            phaseC_dve = dve.n

        y_acc = sb("y_acc", [128, 16, D], BF16)
        with ExitStack() as esD:
            sbD = lambda n_, sh, dt: esD.enter_context(nc.sbuf_tensor(n_, sh, dt))
            w1c = [sbD(f"w1c{i}", [128, NKT, 256], BF16) for i in range(2)]
            w2e = sbD("w2e", [128, NCI, D], BF16)
            xst = [sbD(f"xst{i}", [128, D], BF16) for i in range(2)]
            xsT = sbD("xsT", [128, NKT, 6 * 128], BF16)
            yact = sbD("yact", [128, NCI, 6 * 128], BF16)
            sact = sbD("sact", [128, 6 * 128], BF16)
            ysb = sbD("ysb", [128, D], BF16)
            ye = [sbD(f"ye{i}", [128, D], BF16) for i in range(2)]
            ps_g = esD.enter_context(nc.psum_tensor("ps_g", [128, 6 * 128], F32))
            ps_u = esD.enter_context(nc.psum_tensor("ps_u", [128, 6 * 128], F32))
            ps_y = [esD.enter_context(nc.psum_tensor(f"ps_y{i}", [128, 512], F32))
                    for i in range(2)]
            psTd = esD.enter_context(nc.psum_tensor("psTd", [128, 512], BF16))

            entryD_act, entryD_dve, entryD_pe = act.n, dve.n, pe.n
            p.wait("gpsimd", gld, scat_done[1])
            p.wait("gpsimd", pe, pe.n)
            p.wait("gpsimd", act, act.n)
            p.wait("gpsimd", dve, dve.n)
            p.wait("gpsimd", st, st.n)
            p.op("vector", lambda v: v.memset(ye[0][:], 0.0))
            p.op("vector", lambda v: v.memset(ye[1][:], 0.0), dve)
            ye_zero = dve.n

            w1_ld = [0, 0]
            w1_user = [pe.n, pe.n]
            w2_user = pe.n
            w2_ld = 0
            xst_ld = [0, 0]
            xst_user = [pe.n, pe.n]
            xsT_user = pe.n
            psTd_rd = dve.n
            ps_g_rd = act.n
            ps_u_rd = dve.n
            ps_y_rd = [act.n, act.n]
            ysb_st = 0
            ye_user = [ye_zero, ye_zero]
            firstD = True

            def load_w1(e, ci):
                wb = ci % 2
                p.wait("gpsimd", pe, w1_user[wb])
                p.op("gpsimd", lambda g, e=e, ci=ci, b=wb: g.dma_start(
                    out=w1c[b][:, :, 0:128],
                    in_=w1b_dram.ap()[e, :, ci * 128:(ci + 1) * 128]
                    .rearrange("(k p) m -> p k m", p=128)), gld, 16)
                p.op("gpsimd", lambda g, e=e, ci=ci, b=wb: g.dma_start(
                    out=w1c[b][:, :, 128:256],
                    in_=w1b_dram.ap()[e, :, IFF + ci * 128:IFF + (ci + 1) * 128]
                    .rearrange("(k p) m -> p k m", p=128)), gld, 16)
                w1_ld[wb] = gld.n

            for e in range(E):
                for ci in range(NCI):
                    if ci == 0:
                        p.wait("sync", pe, w2_user)
                        if e == 0:
                            p.wait("sync", gld, wcast_done)
                    p.op("sync", lambda sy, e=e, ci=ci: sy.dma_start(
                        out=w2e[:, ci, :],
                        in_=w2b_dram.ap()[e, ci * 128:(ci + 1) * 128, :]), sld, 16)
                    w2_ld = sld.n
                load_w1(e, 0)
                load_w1(e, 1)
                for g6 in range(6):
                    hh2, rt = g6 // 3, g6 % 3
                    xb2 = g6 % 2
                    row0 = hh2 * (E * CAP) + e * CAP + rt * 128
                    p.wait("sync", gld, scat_done[hh2])
                    p.wait("sync", pe, xst_user[xb2])
                    xst_ld[xb2] = p.op("sync", lambda sy, r0=row0, b=xb2: sy.dma_start(
                        out=xst[b][:], in_=xs_dram.ap()[r0:r0 + 128, :]), sld, 16)
                    p.wait("tensor", sld, xst_ld[xb2])
                    if firstD:
                        p.wait("tensor", act, entryD_act)
                        p.wait("tensor", dve, entryD_dve)
                        firstD = False
                    p.wait("tensor", pe, xsT_user)
                    for ktg in range(4):
                        p.wait("tensor", dve, psTd_rd)
                        for i in range(4):
                            p.op("tensor", lambda t, ktg=ktg, i=i, b=xb2: t.transpose(
                                psTd[:, i * 128:(i + 1) * 128],
                                xst[b][:, (ktg * 4 + i) * 128:(ktg * 4 + i + 1) * 128],
                                ident_sb[:]), pe if i == 3 else None)
                        trd = pe.n
                        if ktg == 3:
                            xst_user[xb2] = trd
                        p.wait("vector", pe, trd)
                        dstX = xsT[:, ktg * 4:(ktg + 1) * 4, g6 * 128:(g6 + 1) * 128]
                        srcX = psTd[:].rearrange("p (k x) -> p k x", k=4)
                        p.op("vector", lambda v, d=dstX, s_=srcX: v.tensor_copy(d, s_), dve)
                        psTd_rd = dve.n
                xsT_done = dve.n
                if dbg and e == 0:
                    p.wait("sync", dve, xsT_done)
                    xsT_st = p.op("sync", lambda sy: sy.dma_start(out=dbg_xsT, in_=xsT[:]), st, 16)
                    p.wait("vector", st, xsT_st)
                    p.wait("tensor", st, xsT_st)
                for ci in range(NCI):
                    wb = ci % 2
                    p.wait("tensor", gld, w1_ld[wb])
                    p.wait("tensor", dve, xsT_done)
                    p.wait("tensor", act, ps_g_rd)
                    for (o_, n_) in ((0, 512), (512, 256)):
                        for kt in range(NKT):
                            p.op("tensor", lambda t, b=wb, kt=kt, o_=o_, n_=n_: t.matmul(
                                ps_g[:, o_:o_ + n_],
                                w1c[b][:, kt, 0:128],
                                xsT[:, kt, o_:o_ + n_],
                                start=(kt == 0), stop=(kt == NKT - 1)),
                                pe if (o_ == 512 and kt == NKT - 1) else None)
                    g_done = pe.n
                    p.wait("tensor", dve, ps_u_rd)
                    for (o_, n_) in ((0, 512), (512, 256)):
                        for kt in range(NKT):
                            p.op("tensor", lambda t, b=wb, kt=kt, o_=o_, n_=n_: t.matmul(
                                ps_u[:, o_:o_ + n_],
                                w1c[b][:, kt, 128:256],
                                xsT[:, kt, o_:o_ + n_],
                                start=(kt == 0), stop=(kt == NKT - 1)),
                                pe if (o_ == 512 and kt == NKT - 1) else None)
                    u_done = pe.n
                    w1_user[wb] = pe.n
                    if ci + 2 < NCI:
                        load_w1(e, ci + 2)
                    elif e + 1 < E and ci + 2 < NCI + 2:
                        load_w1(e + 1, (ci + 2) % 2) if False else None
                    p.op("scalar", lambda s_, v=g_done: s_.wait_ge(pe.h, v))
                    p.op("scalar", lambda s_, v=ps_u_rd: s_.wait_ge(dve.h, v))
                    p.op("scalar", lambda s_: s_.activation(sact[:], ps_g[:], AF.Silu), act)
                    ps_g_rd = act.n
                    silu_done = act.n
                    p.wait("vector", pe, u_done)
                    p.wait("vector", act, silu_done)
                    p.op("vector", lambda v, ci=ci: v.tensor_mul(
                        yact[:, ci, :], sact[:], ps_u[:]), dve)
                    ps_u_rd = dve.n
                yact_done = dve.n
                if dbg and e == 0:
                    p.wait("sync", dve, yact_done)
                    ya2_st = p.op("sync", lambda sy: sy.dma_start(out=dbg_ya2, in_=yact[:]), st, 16)
                    p.wait("tensor", st, ya2_st)
                    p.wait("vector", st, ya2_st)
                xsT_user = pe.n
                for g6 in range(6):
                    hh2, rt = g6 // 3, g6 % 3
                    row0 = hh2 * (E * CAP) + e * CAP + rt * 128
                    for n4 in range(4):
                        yb = n4 % 2
                        p.wait("tensor", gld, w2_ld)
                        p.wait("tensor", dve, yact_done)
                        p.wait("tensor", act, ps_y_rd[yb])
                        for ci in range(NCI):
                            p.op("tensor", lambda t, ci=ci, g6=g6, n4=n4, b=yb: t.matmul(
                                ps_y[b][:],
                                yact[:, ci, g6 * 128:(g6 + 1) * 128],
                                w2e[:, ci, n4 * 512:(n4 + 1) * 512],
                                start=(ci == 0), stop=(ci == NCI - 1)),
                                pe if ci == NCI - 1 else None)
                        y2_done = pe.n
                        p.op("scalar", lambda s_, v=y2_done: s_.wait_ge(pe.h, v))
                        if n4 <= 1:
                            p.op("scalar", lambda s_, v=ysb_st: s_.wait_ge(st.h, v))
                        p.op("scalar", lambda s_, n4=n4, b=yb: s_.copy(
                            ysb[:, n4 * 512:(n4 + 1) * 512], ps_y[b][:]), act)
                        ps_y_rd[yb] = act.n
                    ys_evac = act.n
                    p.op("scalar", lambda s_, v=ys_evac: s_.wait_ge(act.h, v))
                    ysb_st = p.op("scalar", lambda s_, r0=row0: s_.dma_start(
                        out=ys_dram.ap()[r0:r0 + 128, :], in_=ysb[:]), st, 16)
                w2_user = pe.n
                ys_done = ysb_st
                p.wait("gpsimd", st, ys_done)
                for tt in range(16):
                    gb = tt % 2
                    p.wait("gpsimd", dve, ye_user[gb])
                    gat = p.op("gpsimd", lambda g, tt=tt, e=e, b=gb: g.indirect_dma_start(
                        ye[b][:], None,
                        ys_dram.ap()[0:128, :],
                        IndirectOffsetOnAxis(slot_sb[:, tt, e:e + 1], 1),
                        bounds_check=bregs["gat"], oob_is_err=False), gld, 16)
                    p.wait("vector", gld, gat)
                    gap = g_sb[:, tt, e:e + 1]
                    if e == 0:
                        p.op("vector", lambda v, tt=tt, b=gb, gap=gap: v.tensor_scalar(
                            y_acc[:, tt, :], ye[b][:], gap, None, ALU.mult), dve)
                    else:
                        p.op("vector", lambda v, tt=tt, b=gb, gap=gap: v.scalar_tensor_tensor(
                            y_acc[:, tt, :], ye[b][:], gap, y_acc[:, tt, :],
                            ALU.mult, ALU.add), dve)
                    ye_user[gb] = dve.n
            y_done = dve.n
            lastD_act = act.n

        with ExitStack() as esE:
            sbE = lambda n_, sh, dt: esE.enter_context(nc.sbuf_tensor(n_, sh, dt))
            hE = [sbE(f"hE{i}", [128, D], BF16) for i in range(2)]
            hp2 = sbE("hp2", [128, D], F32)
            outt = [sbE(f"outt{i}", [128, D], F32) for i in range(2)]
            sqE = sbE("sqE", [128, D], BF16)
            sum1e = sbE("sum1e", [128, 1], F32)
            ssqe = sbE("ssqe", [128, 1], F32)
            negme = sbE("negme", [128, 1], F32)
            m2te = sbE("m2te", [128, 1], F32)
            vare = sbE("vare", [128, 1], F32)
            srte = sbE("srte", [128, 1], F32)
            rstde = sbE("rstde", [128, 1], F32)

            hE_ld = [0, 0]
            hE_user = [0, 0]
            hp2_sq = act.n
            out_user = [st.n, st.n]
            p.wait("sync", pe, pe.n)
            p.wait("sync", dve, y_done)
            p.wait("sync", act, lastD_act)
            p.wait("sync", gld, gld.n)
            for tt in range(16):
                ob = tt % 2
                p.wait("sync", st, h_st)
                p.wait("sync", dve, hE_user[ob])
                hE_ld[ob] = p.op("sync", lambda sy, tt=tt, b=ob: sy.dma_start(
                    out=hE[b][:], in_=h_tok_dram.ap()[tt * 128:(tt + 1) * 128, :]), sld, 16)
                p.wait("vector", sld, hE_ld[ob])
                p.wait("vector", dve, y_done)
                p.wait("vector", act, hp2_sq)
                p.op("vector", lambda v, tt=tt, b=ob: v.tensor_add(
                    hp2[:], hE[b][:], y_acc[:, tt, :]), dve)
                c1_ = dve.n
                hE_user[ob] = c1_
                p.op("scalar", lambda s_, v=c1_: s_.wait_ge(dve.h, v))
                p.op("scalar", lambda s_: s_.activation(
                    sqE[:], hp2[:], AF.Square, accum_out=ssqe[:]), act)
                hp2_sq = act.n
                sqe_done = act.n
                p.wait("vector", dve, c1_)
                p.op("vector", lambda v: v.reduce_sum(sum1e[:], hp2[:], axis=AX.X), dve)
                c2_ = dve.n
                p.wait("vector", dve, c2_)
                p.op("vector", lambda v: v.tensor_scalar(
                    negme[:], sum1e[:], -1.0 / D, None, ALU.mult), dve)
                c3_ = dve.n
                p.wait("vector", dve, c3_)
                p.op("vector", lambda v: v.tensor_mul(m2te[:], negme[:], negme[:]))
                p.wait("vector", act, sqe_done)
                p.op("vector", lambda v: v.tensor_scalar(
                    vare[:], ssqe[:], 1.0 / D, None, ALU.mult), dve)
                c4_ = dve.n
                p.wait("vector", dve, c4_)
                p.op("vector", lambda v: v.tensor_sub(vare[:], vare[:], m2te[:]), dve)
                c5_ = dve.n
                p.op("scalar", lambda s_, v=c5_: s_.wait_ge(dve.h, v))
                p.op("scalar", lambda s_: s_.activation(
                    srte[:], vare[:], AF.Sqrt, bias=eps_sb[:]), act)
                c6_ = act.n
                p.wait("vector", act, c6_)
                p.op("vector", lambda v: v.reciprocal(rstde[:], srte[:]), dve)
                c7_ = dve.n
                p.wait("vector", dve, c7_)
                p.wait("vector", st, out_user[ob])
                p.op("vector", lambda v, b=ob: v.tensor_scalar(
                    outt[b][:], hp2[:], negme[:], rstde[:], ALU.add, ALU.mult), dve)
                c8_ = dve.n
                p.wait("scalar", dve, c8_)
                p.op("scalar", lambda s_, tt=tt, b=ob: s_.dma_start(
                    out=out[tt * 128:(tt + 1) * 128, :], in_=outt[b][:]), st, 16)
                out_user[ob] = st.n
        out_st_total = st.n

        if dbg:
            p.wait("sync", cc, cc_done)
            p.wait("sync", st, out_st_total)
            p.op("sync", lambda sy: sy.dma_start(out=dbg_attn, in_=attn_gat.ap()), st, 16)
            p.op("sync", lambda sy: sy.dma_start(
                out=dbg_h.rearrange("t d -> t d"), in_=h_tok_dram.ap()), st, 16)
            p.op("sync", lambda sy: sy.dma_start(out=dbg_g, in_=g_sb[:]), st, 16)
            p.op("sync", lambda sy: sy.dma_start(out=dbg_s12, in_=slot12_sb[:]), st, 16)
            p.op("sync", lambda sy: sy.dma_start(out=dbg_slot, in_=slot_sb[:]), st, 16)
            p.op("sync", lambda sy: sy.dma_start(out=dbg_xs, in_=xs_dram.ap()), st, 16)
            p.op("sync", lambda sy: sy.dma_start(out=dbg_ys, in_=ys_dram.ap()), st, 16)
            p.op("sync", lambda sy: sy.dma_start(out=dbg_yacc, in_=y_acc[:]), st, 16)

        p.wait("sync", st, st.n)

        block = es.enter_context(nc.Block())

        @block.tensor
        def _(t):
            for fn in p.streams["tensor"]:
                fn(t)

        @block.vector
        def _(v):
            for fn in p.streams["vector"]:
                fn(v)

        @block.scalar
        def _(s_):
            for fn in p.streams["scalar"]:
                fn(s_)

        @block.gpsimd
        def _(g):
            for fn in p.streams["gpsimd"]:
                fn(g)

        @block.sync
        def _(sy):
            for fn in p.streams["sync"]:
                fn(sy)

    return nc


_ROPE_PERM = np.concatenate([np.arange(0, 128, 2), np.arange(1, 128, 2)])


def _host_prepare(x, Wq, Wk, Wv, Wo, router_w, w1, w2):
    x = np.asarray(x, dtype=np.float32)
    xT = [np.ascontiguousarray(x[b].T) for b in range(2)]
    wq_s, wk_s, wv_s = [], [], []
    for hg in range(4):
        def permslice(W):
            blk = np.asarray(W)[hg * 512:(hg + 1) * 512].reshape(4, 128, D)[:, _ROPE_PERM, :]
            return np.ascontiguousarray(blk.reshape(512, D).T)
        wq_s.append(permslice(Wq))
        wk_s.append(permslice(Wk))
        wv_s.append(np.ascontiguousarray(
            np.asarray(Wv)[hg * 512:(hg + 1) * 512].T))
    woT = np.ascontiguousarray(np.asarray(Wo).T)
    routerT = np.ascontiguousarray(np.asarray(router_w).T)
    w1T = np.ascontiguousarray(np.asarray(w1).transpose(0, 2, 1))
    w2T = np.ascontiguousarray(np.asarray(w2).transpose(0, 2, 1))
    inv_freq = (1.0 / (10000.0 ** (np.arange(0, 128, 2, dtype=np.float32) / 128.0))).astype(np.float32)
    t = np.arange(S, dtype=np.float32)
    freqs = np.outer(t, inv_freq).astype(np.float32)
    cosT = np.cos(freqs).astype(np.float32)
    sinT = np.sin(freqs).astype(np.float32)
    ident = np.eye(128, dtype=np.float32)
    kk, qq = np.meshgrid(np.arange(128), np.arange(128), indexing="ij")
    diag = (kk <= qq).astype(np.float32)
    pp = np.arange(128, dtype=np.int64)
    ii, jj = np.meshgrid(np.arange(128), np.arange(128), indexing="ij")
    ltc = (jj >= ii).astype(np.float32)
    sbase = np.zeros((1, 128), dtype=np.float32)
    for tile in range(16):
        for e in range(8):
            sbase[0, tile * 8 + e] = (tile // 8) * (8 * 384) + e * 384
    in_maps = []
    for c in range(8):
        b, hg = c // 4, c % 4
        tok_off = 2048 * hg
        half = tok_off // 4096
        agidx = np.stack(
            [(half * 2048 + kt * 128 + pp) * 4096 + (tok_off % 4096)
             for kt in range(NKT)], axis=1).astype(np.int32)
        in_maps.append({
            "xT": xT[b],
            "x_own": np.ascontiguousarray(x[b, tok_off:tok_off + TOK]),
            "wqT": wq_s[hg], "wkT": wk_s[hg], "wvT": wv_s[hg],
            "woT": woT, "routerT": routerT, "w1T": w1T, "w2T": w2T,
            "cosT": cosT, "sinT": sinT, "identD": ident, "diagD": diag,
            "agidx": agidx, "ltcD": ltc, "sbaseD": sbase,
        })
    return in_maps


_NC_CACHE = {}


def kernel(L0_attn_x, L0_attn_Wq, L0_attn_Wk, L0_attn_Wv, L0_attn_Wo,
           L0_ffn_router_w, L0_ffn_w1, L0_ffn_w2, _dbg=False):
    in_maps = _host_prepare(L0_attn_x, L0_attn_Wq, L0_attn_Wk, L0_attn_Wv,
                            L0_attn_Wo, L0_ffn_router_w, L0_ffn_w1, L0_ffn_w2)
    key = bool(_dbg)
    if key not in _NC_CACHE:
        _NC_CACHE[key] = build(dbg=_dbg)
    nc = _NC_CACHE[key]
    res = run_bass_kernel_spmd(nc, in_maps, core_ids=list(range(8)))
    out = np.empty((2, S, D), dtype=np.float32)
    for c in range(8):
        b, hg = c // 4, c % 4
        out[b, 2048 * hg:2048 * (hg + 1)] = res.results[c]["out"]
    if _dbg:
        return out, res
    return out


# version 06
# speedup vs baseline: 1.0433x; 1.0433x over previous
import math
from contextlib import ExitStack

import numpy as np
import ml_dtypes
import concourse.bass as bass
import concourse.mybir as mybir
from concourse.ap import AP
from concourse.bass_core_rust import IndirectOffsetOnAxis
from concourse.bass_utils import run_bass_kernel_spmd

F32 = mybir.dt.float32
BF16 = mybir.dt.bfloat16
I32 = mybir.dt.int32
AF = mybir.ActivationFunctionType
ALU = mybir.AluOpType
AX = mybir.AxisListType

S = 8192
D = 2048
DH = 128
NHC = 4
E = 8
IFF = 1408
TOK = 2048
NKT = D // 128
NCI = IFF // 128
CAP = 384
NSL = 2 * E * CAP
RSQ = 1.0 / math.sqrt(128.0)

ENGINES = ("tensor", "vector", "scalar", "gpsimd", "sync")


class Sem:
    __slots__ = ("h", "n")


class Plan:
    def __init__(self, nc, es):
        self.nc = nc
        self.es = es
        self.streams = {k: [] for k in ENGINES}

    def sem(self, name):
        s = Sem()
        s.h = self.es.enter_context(self.nc.semaphore(name))
        s.n = 0
        return s

    def op(self, eng, fn, inc=None, k=1):
        if inc is not None:
            inc.n += k

            def run(EE, fn=fn, h=inc.h, k=k):
                fn(EE).then_inc(h, k)
        else:
            def run(EE, fn=fn):
                fn(EE)
        self.streams[eng].append(run)
        return inc.n if inc is not None else None

    def wait(self, eng, sem, v=None):
        v = sem.n if v is None else v
        if v <= 0:
            return
        self.streams[eng].append(lambda EE, h=sem.h, v=v: EE.wait_ge(h, v))


def build(dbg=False):
    nc = bass.Bass("TRN2", target_bir_lowering=False)

    xT = nc.dram_tensor("xT", [D, S], F32, kind="ExternalInput").ap()
    x_own = nc.dram_tensor("x_own", [TOK, D], F32, kind="ExternalInput").ap()
    wqT = nc.dram_tensor("wqT", [D, NHC * DH], F32, kind="ExternalInput").ap()
    wkT = nc.dram_tensor("wkT", [D, NHC * DH], F32, kind="ExternalInput").ap()
    wvT = nc.dram_tensor("wvT", [D, NHC * DH], F32, kind="ExternalInput").ap()
    woT = nc.dram_tensor("woT", [D, D], F32, kind="ExternalInput").ap()
    routerT = nc.dram_tensor("routerT", [D, E], F32, kind="ExternalInput").ap()
    w1P = nc.dram_tensor("w1P", [E, NCI, 128, NKT, 256], F32, kind="ExternalInput").ap()
    w2T = nc.dram_tensor("w2T", [E, IFF, D], F32, kind="ExternalInput").ap()
    cosT = nc.dram_tensor("cosT", [S, 64], F32, kind="ExternalInput").ap()
    sinT = nc.dram_tensor("sinT", [S, 64], F32, kind="ExternalInput").ap()
    identD = nc.dram_tensor("identD", [128, 128], F32, kind="ExternalInput").ap()
    diagD = nc.dram_tensor("diagD", [128, 128], F32, kind="ExternalInput").ap()
    agidx = nc.dram_tensor("agidx", [128, NKT], I32, kind="ExternalInput").ap()
    ltcD = nc.dram_tensor("ltcD", [128, 128], F32, kind="ExternalInput").ap()
    sbaseD = nc.dram_tensor("sbaseD", [1, 128], F32, kind="ExternalInput").ap()

    out = nc.dram_tensor("out", [TOK, D], F32, kind="ExternalOutput").ap()

    attn_self = nc.dram_tensor("attn_self", [2 * NHC * DH, S // 2], BF16)
    attn_gat = nc.dram_tensor("attn_gat", [2 * 16 * DH, S // 2], BF16)
    h_tok_dram = nc.dram_tensor("h_tok_dram", [TOK, D], BF16)
    xs_dram = nc.dram_tensor("xs_dram", [2 * E * 384, D], BF16)
    ys_dram = nc.dram_tensor("ys_dram", [2 * E * 384, D], BF16)
    groups = [[0, 1, 2, 3], [4, 5, 6, 7]]

    if dbg:
        dbg_attn = nc.dram_tensor("dbg_attn", [16 * DH, S], BF16, kind="ExternalOutput").ap()
        dbg_h = nc.dram_tensor("dbg_h", [TOK, D], BF16, kind="ExternalOutput").ap()
        dbg_g = nc.dram_tensor("dbg_g", [128, 16, E], F32, kind="ExternalOutput").ap()
        dbg_den = nc.dram_tensor("dbg_den", [32, 1024], F32, kind="ExternalOutput").ap()
        dbg_s12 = nc.dram_tensor("dbg_s12", [128, 16, 2], I32, kind="ExternalOutput").ap()
        dbg_slot = nc.dram_tensor("dbg_slot", [128, 16, E], I32, kind="ExternalOutput").ap()
        dbg_xs = nc.dram_tensor("dbg_xs", [NSL, D], BF16, kind="ExternalOutput").ap()
        dbg_ys = nc.dram_tensor("dbg_ys", [NSL, D], BF16, kind="ExternalOutput").ap()
        dbg_yacc = nc.dram_tensor("dbg_yacc", [128, 16, D], BF16, kind="ExternalOutput").ap()
        dbg_xsT = nc.dram_tensor("dbg_xsT", [128, NKT, 768], BF16, kind="ExternalOutput").ap()
        dbg_ya2 = nc.dram_tensor("dbg_ya2", [128, NCI, 768], BF16, kind="ExternalOutput").ap()
        dbg_qT = nc.dram_tensor("dbg_qT", [NHC * 128, 1024], BF16, kind="ExternalOutput").ap()
        dbg_y = nc.dram_tensor("dbg_y", [128, 8, D], BF16, kind="ExternalOutput").ap()
        dbg_ya = nc.dram_tensor("dbg_ya", [128, NCI, 1024], BF16, kind="ExternalOutput").ap()
        dbg_pt = nc.dram_tensor("dbg_pt", [128, 1024], BF16, kind="ExternalOutput").ap()

    es = ExitStack()
    with es:
        p = Plan(nc, es)
        gld = p.sem("gld")
        sld = p.sem("sld")
        pe = p.sem("pe")
        act = p.sem("act")
        dve = p.sem("dve")
        st = p.sem("st")
        cc = p.sem("cc")

        sb = lambda name, shape, dt: es.enter_context(nc.sbuf_tensor(name, shape, dt))

        ident_sb = sb("ident_sb", [128, 128], BF16)
        diag_sb = sb("diag_sb", [128, 128], BF16)
        ones_col = sb("ones_col", [128, 1], BF16)
        ones_row = sb("ones_row", [1, 128], F32)
        ones_rowb = sb("ones_rowb", [1, 128], BF16)
        eps_sb = sb("eps_sb", [128, 1], F32)
        routerT_sb = sb("routerT_sb", [128, NKT, E], BF16)
        g_sb = sb("g_sb", [128, 16, E], F32)
        agidx_sb = sb("agidx_sb", [128, NKT], I32)
        ltc_sb = sb("ltc_sb", [128, 128], BF16)
        sbase_sb = sb("sbase_sb", [1, 128], F32)
        slot_sb = sb("slot_sb", [128, 16, E], I32)
        slot12_sb = sb("slot12_sb", [128, 16, 2], I32)

        p.op("gpsimd", lambda g: g.dma_start(out=ident_sb[:], in_=identD), gld, 16)
        p.op("gpsimd", lambda g: g.dma_start(out=diag_sb[:], in_=diagD), gld, 16)
        p.op("gpsimd", lambda g: g.dma_start(
            out=routerT_sb[:], in_=routerT.rearrange("(k p) e -> p k e", p=128)), gld, 16)
        p.op("gpsimd", lambda g: g.memset(ones_col[:], 1.0))
        p.op("gpsimd", lambda g: g.memset(ones_row[:], 1.0))
        p.op("gpsimd", lambda g: g.memset(ones_rowb[:], 1.0))
        p.op("gpsimd", lambda g: g.memset(eps_sb[:], 1e-5))
        p.op("sync", lambda sy: sy.dma_start(out=agidx_sb[:], in_=agidx), sld, 16)
        p.op("gpsimd", lambda g: g.dma_start(out=ltc_sb[:], in_=ltcD), gld, 16)
        p.op("sync", lambda sy: sy.dma_start(out=sbase_sb[:], in_=sbaseD), sld, 16)
        consts_gld = gld.n

        esAB = ExitStack()
        kT_sb = esAB.enter_context(nc.sbuf_tensor("kT_sb", [128, NHC, S], BF16))
        v_sb = esAB.enter_context(nc.sbuf_tensor("v_sb", [128, S // 128, NHC * DH], BF16))

        with ExitStack() as esA:
            sbA = lambda n_, sh, dt: esA.enter_context(nc.sbuf_tensor(n_, sh, dt))
            wk_sb = sbA("wk_sb", [128, NKT, NHC * DH], BF16)
            wv_sb = sbA("wv_sb", [128, NKT, NHC * DH], BF16)
            xTc = [sbA(f"xTc{i}", [128, NKT, 512], BF16) for i in range(2)]
            cos_c = [sbA(f"cos_c{i}", [128, 4, 64], F32) for i in range(2)]
            sin_c = [sbA(f"sin_c{i}", [128, 4, 64], F32) for i in range(2)]
            t_scr = [sbA(f"t_scr{i}", [128, 4, 64], F32) for i in range(4)]
            ktmp = sbA("ktmp", [128, 512], BF16)
            psK = esA.enter_context(nc.psum_tensor("psK", [128, 512], F32))
            psV = esA.enter_context(nc.psum_tensor("psV", [128, 512], F32))
            psT = esA.enter_context(nc.psum_tensor("psT", [128, 512], BF16))

            p.op("gpsimd", lambda g: g.dma_start(
                out=wk_sb[:], in_=wkT.rearrange("(k p) m -> p k m", p=128)), gld, 16)
            p.op("gpsimd", lambda g: g.dma_start(
                out=wv_sb[:], in_=wvT.rearrange("(k p) m -> p k m", p=128)), gld, 16)
            wA_gld = gld.n

            xTc_user = [0, 0]
            cos_user = [0, 0]
            xTc_ld = [0, 0]
            cos_ld = [0, 0]
            psK_rd = 0
            psT_rd = 0
            psV_rd = 0
            ktmp_rd = 0

            for cch in range(16):
                bf = cch % 2
                p.wait("gpsimd", pe, xTc_user[bf])
                xTc_ld[bf] = p.op("gpsimd", lambda g, c=cch, b=bf: g.dma_start(
                    out=xTc[b][:], in_=xT[:, c * 512:(c + 1) * 512]
                    .rearrange("(k p) s -> p k s", p=128)), gld, 16)

                p.wait("sync", dve, cos_user[bf])
                p.op("sync", lambda sy, c=cch, b=bf: sy.dma_start(
                    out=cos_c[b][:], in_=cosT[c * 512:(c + 1) * 512, :]
                    .rearrange("(t p) f -> p t f", p=128)), sld, 16)
                cos_ld[bf] = p.op("sync", lambda sy, c=cch, b=bf: sy.dma_start(
                    out=sin_c[b][:], in_=sinT[c * 512:(c + 1) * 512, :]
                    .rearrange("(t p) f -> p t f", p=128)), sld, 16)

                for stl in range(4):
                    g_tile = cch * 4 + stl
                    p.wait("tensor", gld, max(xTc_ld[bf], wA_gld))
                    p.wait("tensor", dve, psK_rd)
                    for kt in range(NKT):
                        p.op("tensor", lambda t, b=bf, kt=kt, s=stl: t.matmul(
                            psK[:], xTc[b][:, kt, s * 128:(s + 1) * 128],
                            wk_sb[:, kt, :], start=(kt == 0), stop=(kt == NKT - 1)),
                            pe if kt == NKT - 1 else None)
                    peK = pe.n
                    p.wait("tensor", act, psV_rd)
                    for kt in range(NKT):
                        p.op("tensor", lambda t, b=bf, kt=kt, s=stl: t.matmul(
                            psV[:], xTc[b][:, kt, s * 128:(s + 1) * 128],
                            wv_sb[:, kt, :], start=(kt == 0), stop=(kt == NKT - 1)),
                            pe if kt == NKT - 1 else None)
                    peV = pe.n
                    p.wait("vector", pe, peK)
                    p.wait("vector", sld, cos_ld[bf])
                    p.wait("vector", pe, ktmp_rd)
                    ps3 = psK[:].rearrange("p (h x) -> p h x", h=4)
                    x1, x2 = ps3[:, :, 0:64], ps3[:, :, 64:128]
                    cb = AP(cos_c[bf], stl * 64, [[256, 128], [0, 4], [1, 64]])
                    sbn = AP(sin_c[bf], stl * 64, [[256, 128], [0, 4], [1, 64]])
                    p.op("vector", lambda v, x1=x1, cb=cb: v.tensor_mul(t_scr[0][:], x1, cb))
                    p.op("vector", lambda v, x2=x2, sbn=sbn: v.tensor_mul(t_scr[1][:], x2, sbn))
                    p.op("vector", lambda v, x1=x1, sbn=sbn: v.tensor_mul(t_scr[2][:], x1, sbn))
                    p.op("vector", lambda v, x2=x2, cb=cb: v.tensor_mul(t_scr[3][:], x2, cb), dve)
                    psK_rd = dve.n
                    p.wait("vector", dve, psK_rd)
                    kt3 = ktmp[:].rearrange("p (h x) -> p h x", h=4)
                    p.op("vector", lambda v, kt3=kt3: v.tensor_sub(
                        kt3[:, :, 0:64], t_scr[0][:], t_scr[1][:]))
                    p.op("vector", lambda v, kt3=kt3: v.tensor_add(
                        kt3[:, :, 64:128], t_scr[2][:], t_scr[3][:]), dve)
                    kt_ready = dve.n
                    p.op("scalar", lambda s_, v=peV: s_.wait_ge(pe.h, v))
                    p.op("scalar", lambda s_, g=g_tile: s_.copy(v_sb[:, g, :], psV[:]), act)
                    psV_rd = act.n
                    p.wait("tensor", dve, max(kt_ready, psT_rd))
                    for h in range(NHC):
                        p.op("tensor", lambda t, h=h: t.transpose(
                            psT[:, h * 128:(h + 1) * 128],
                            ktmp[:, h * 128:(h + 1) * 128], ident_sb[:]),
                            pe if h == NHC - 1 else None)
                    ktmp_rd = pe.n
                    peT = pe.n
                    p.wait("vector", pe, peT)
                    dstK = kT_sb[:, :, g_tile * 128:(g_tile + 1) * 128]
                    srcT = psT[:].rearrange("p (h x) -> p h x", h=4)
                    p.op("vector", lambda v, d=dstK, s_=srcT: v.tensor_copy(d, s_), dve)
                    psT_rd = dve.n
                xTc_user[bf] = pe.n
                cos_user[bf] = dve.n
            phaseA_pe = pe.n

        with ExitStack() as esB:
            sbB = lambda n_, sh, dt: esB.enter_context(nc.sbuf_tensor(n_, sh, dt))
            wq_sb = sbB("wq_sb", [128, NKT, NHC * DH], BF16)
            xTI = sbB("xTI", [128, NKT, 1024], BF16)
            cos_I = sbB("cos_I", [128, 8, 64], F32)
            sin_I = sbB("sin_I", [128, 8, 64], F32)
            tB = [sbB(f"tB{i}", [128, 4, 64], F32) for i in range(2)]
            qtmp = sbB("qtmp", [128, 512], BF16)
            qT_sb = sbB("qT_sb", [128, NHC, 1024], BF16)
            pt = [sbB(f"pt{i}", [128, 1024], BF16) for i in range(2)]
            rb_sb = sbB("rb_sb", [128, 1024], F32)
            rdb_sb = sbB("rdb_sb", [1, 1024], BF16)
            o_sb = sbB("o_sb", [128, 1024], BF16)

            p.wait("gpsimd", pe, phaseA_pe)
            p.op("gpsimd", lambda g: g.dma_start(
                out=wq_sb[:], in_=wqT.rearrange("(k p) m -> p k m", p=128)), gld, 16)
            wq_ld = gld.n
            p.wait("sync", pe, phaseA_pe)

            xTI_user = 0
            cosI_user = 0
            o_store = [0]
            attn_half_st = 0
            for I in range(8):
                p.wait("gpsimd", pe, xTI_user)
                xTI_ld = p.op("gpsimd", lambda g, I=I: g.dma_start(
                    out=xTI[:], in_=xT[:, I * 1024:(I + 1) * 1024]
                    .rearrange("(k p) s -> p k s", p=128)), gld, 16)
                p.wait("sync", dve, cosI_user)
                p.op("sync", lambda sy, I=I: sy.dma_start(
                    out=cos_I[:], in_=cosT[I * 1024:(I + 1) * 1024, :]
                    .rearrange("(t p) f -> p t f", p=128)), sld, 16)
                cosI_ld = p.op("sync", lambda sy, I=I: sy.dma_start(
                    out=sin_I[:], in_=sinT[I * 1024:(I + 1) * 1024, :]
                    .rearrange("(t p) f -> p t f", p=128)), sld, 16)

                superA_act, superA_dve = act.n, dve.n

                with ExitStack() as esQ:
                    psQ = esQ.enter_context(nc.psum_tensor(f"psQ{I}", [128, 512], F32))
                    psTq = esQ.enter_context(nc.psum_tensor(f"psTq{I}", [128, 512], BF16))
                    psQ_rd = 0
                    psTq_rd = 0
                    qtmp_rd = 0
                    first_q = True
                    for stl in range(8):
                        p.wait("tensor", gld, max(xTI_ld, wq_ld))
                        if first_q:
                            p.wait("tensor", act, superA_act)
                            p.wait("tensor", dve, superA_dve)
                            first_q = False
                        p.wait("tensor", dve, psQ_rd)
                        for kt in range(NKT):
                            p.op("tensor", lambda t, kt=kt, s=stl: t.matmul(
                                psQ[:], xTI[:, kt, s * 128:(s + 1) * 128],
                                wq_sb[:, kt, :], start=(kt == 0), stop=(kt == NKT - 1)),
                                pe if kt == NKT - 1 else None)
                        peQ = pe.n
                        p.wait("vector", pe, max(peQ, qtmp_rd))
                        p.wait("vector", sld, cosI_ld)
                        ps3 = psQ[:].rearrange("p (h x) -> p h x", h=4)
                        x1, x2 = ps3[:, :, 0:64], ps3[:, :, 64:128]
                        cb = AP(cos_I, stl * 64, [[512, 128], [0, 4], [1, 64]])
                        sbn = AP(sin_I, stl * 64, [[512, 128], [0, 4], [1, 64]])
                        p.op("vector", lambda v, x1=x1, cb=cb: v.tensor_mul(tB[0][:], x1, cb))
                        p.op("vector", lambda v, x2=x2, sbn=sbn: v.tensor_mul(tB[1][:], x2, sbn), dve)
                        p.wait("vector", dve, dve.n)
                        qt3 = qtmp[:].rearrange("p (h x) -> p h x", h=4)
                        p.op("vector", lambda v, qt3=qt3: v.tensor_sub(
                            qt3[:, :, 0:64], tB[0][:], tB[1][:]))
                        p.op("vector", lambda v, x1=x1, sbn=sbn: v.tensor_mul(tB[0][:], x1, sbn))
                        p.op("vector", lambda v, x2=x2, cb=cb: v.tensor_mul(tB[1][:], x2, cb), dve)
                        psQ_rd = dve.n
                        p.wait("vector", dve, psQ_rd)
                        p.op("vector", lambda v, qt3=qt3: v.tensor_add(
                            qt3[:, :, 64:128], tB[0][:], tB[1][:]), dve)
                        qt_ready = dve.n
                        p.wait("tensor", dve, max(qt_ready, psTq_rd))
                        for h in range(NHC):
                            p.op("tensor", lambda t, h=h: t.transpose(
                                psTq[:, h * 128:(h + 1) * 128],
                                qtmp[:, h * 128:(h + 1) * 128], ident_sb[:]),
                                pe if h == NHC - 1 else None)
                        qtmp_rd = pe.n
                        peTq = pe.n
                        p.wait("vector", pe, peTq)
                        dstQ = qT_sb[:, :, stl * 128:(stl + 1) * 128]
                        srcTq = psTq[:].rearrange("p (h x) -> p h x", h=4)
                        p.op("vector", lambda v, d=dstQ, s_=srcTq: v.tensor_copy(d, s_), dve)
                        psTq_rd = dve.n
                    qT_ready = dve.n
                if dbg and I == 7:
                    p.wait("sync", dve, qT_ready)
                    qT_st = p.op("sync", lambda sy: sy.dma_start(
                        out=dbg_qT.rearrange("(h p) q -> p h q", p=128), in_=qT_sb[:]), st, 16)
                xTI_user = pe.n
                cosI_user = dve.n

                with ExitStack() as esS:
                    ps_sc = [esS.enter_context(nc.psum_tensor(f"ps_sc{I}_{i}", [128, 1024], F32))
                             for i in range(2)]
                    ps_o = esS.enter_context(nc.psum_tensor(f"ps_o{I}", [128, 1024], F32))
                    ps_d = esS.enter_context(nc.psum_tensor(f"ps_d{I}", [1, 1024], F32))

                    sc_rd = [act.n, act.n]
                    pt_rd = [pe.n, pe.n]
                    o_rd = dve.n
                    d_rd = dve.n
                    bc_rd = dve.n
                    njj = 8 * I + 8

                    def emit_av(info):
                        (j, bf, qs, parts, exp_done, mask_done, first, last) = info
                        qoff = qs * 128
                        p.op("tensor", lambda t, v=exp_done: t.wait_ge(act.h, v))
                        if mask_done is not None:
                            p.op("tensor", lambda t, v=mask_done: t.wait_ge(dve.h, v))
                        for (o_, n_) in parts:
                            p.op("tensor", lambda t, b=bf, o_=o_, n_=n_, q=qoff, j=j, f=first, l=last, h=hh: t.matmul(
                                ps_o[:, q + o_:q + o_ + n_],
                                v_sb[:, j, h * 128:(h + 1) * 128],
                                pt[b][:, o_:o_ + n_], start=f, stop=l))
                        for ip, (o_, n_) in enumerate(parts):
                            p.op("tensor", lambda t, b=bf, o_=o_, n_=n_, q=qoff, f=first, l=last: t.matmul(
                                ps_d[0:1, q + o_:q + o_ + n_], ones_col[:],
                                pt[b][:, o_:o_ + n_], start=f, stop=l),
                                pe if ip == len(parts) - 1 else None)
                        pt_rd[bf] = pe.n

                    for hh in range(NHC):
                        pending = None
                        for j in range(njj):
                            bf = j % 2
                            qs = max(j - 8 * I, 0)
                            N = (8 - qs) * 128
                            qoff = qs * 128
                            parts = [(0, min(N, 512))]
                            if N > 512:
                                parts.append((512, N - 512))
                            p.wait("tensor", dve, qT_ready)
                            p.wait("tensor", act, sc_rd[bf])
                            if j <= 1:
                                p.wait("tensor", dve, max(bc_rd, d_rd, o_rd))
                            for ip, (o_, n_) in enumerate(parts):
                                p.op("tensor", lambda t, h=hh, j=j, b=bf, o_=o_, n_=n_, q=qoff: t.matmul(
                                    ps_sc[b][:, o_:o_ + n_],
                                    kT_sb[:, h, j * 128:(j + 1) * 128],
                                    qT_sb[:, h, q + o_:q + o_ + n_],
                                    start=True, stop=True),
                                    pe if ip == len(parts) - 1 else None)
                            sc_done = pe.n
                            if pending is not None:
                                emit_av(pending)
                            p.op("scalar", lambda s_, v=sc_done: s_.wait_ge(pe.h, v))
                            p.op("scalar", lambda s_, v=pt_rd[bf]: s_.wait_ge(pe.h, v))
                            p.op("scalar", lambda s_, b=bf, N=N: s_.activation(
                                pt[b][:, 0:N], ps_sc[b][:, 0:N], AF.Exp, scale=RSQ), act)
                            sc_rd[bf] = act.n
                            exp_done = act.n
                            mask_done = None
                            if dbg and I == 7 and hh == 0 and j == 0:
                                p.wait("sync", act, exp_done)
                                pt_st = p.op("sync", lambda sy, b=bf: sy.dma_start(
                                    out=dbg_pt, in_=pt[b][:]), st, 16)
                                p.wait("scalar", st, pt_st)
                            if j >= 8 * I:
                                p.wait("vector", act, exp_done)
                                p.op("vector", lambda v, b=bf: v.tensor_mul(
                                    pt[b][:, 0:128], pt[b][:, 0:128], diag_sb[:]), dve)
                                mask_done = dve.n
                            pending = (j, bf, qs, parts, exp_done, mask_done,
                                       j == 0, j == njj - 1)
                        emit_av(pending)
                        av_all = pe.n
                        p.wait("vector", pe, av_all)
                        def _recip(v):
                            with nc.allow_low_precision(reason="softmax 1/denom broadcast in bf16 (tolerance budgeted)"):
                                return v.reciprocal(rdb_sb[0:1, :], ps_d[0:1, :])
                        p.op("vector", _recip, dve)
                        d_rd = dve.n
                        if dbg:
                            p.wait("sync", dve, d_rd)
                            den_st = p.op("sync", lambda sy, I=I, h=hh: sy.dma_start(
                                out=dbg_den[I * 4 + h:I * 4 + h + 1, :],
                                in_=rb_sb[0:1, :]), st, 16)
                            p.wait("vector", st, den_st)
                        p.wait("tensor", dve, d_rd)
                        p.wait("tensor", act, sc_rd[1])
                        for ip, o_ in enumerate((0, 512)):
                            p.op("tensor", lambda t, o_=o_: t.matmul(
                                ps_sc[1][:, o_:o_ + 512], ones_rowb[:],
                                rdb_sb[0:1, o_:o_ + 512], start=True, stop=True),
                                pe if ip == 1 else None)
                        bc_done = pe.n
                        p.wait("vector", pe, bc_done)
                        p.op("vector", lambda v: v.tensor_copy(rb_sb[:], ps_sc[1][:]), dve)
                        p.wait("vector", dve, dve.n)
                        p.wait("vector", st, o_store[0])
                        p.op("vector", lambda v: v.tensor_mul(o_sb[:], ps_o[:], rb_sb[:]), dve)
                        o_rd = dve.n
                        bc_rd = o_rd
                        sc_rd[1] = act.n
                        p.wait("scalar", dve, o_rd)
                        p.op("scalar", lambda s_, h=hh, I=I: s_.dma_start(
                            out=attn_self.ap()[(I // 4) * 512 + h * 128:(I // 4) * 512 + (h + 1) * 128,
                                               (I % 4) * 1024:(I % 4 + 1) * 1024],
                            in_=o_sb[:]), st, 16)
                        o_store[0] = st.n
                if I == 3:
                    attn_half_st = st.n
                    p.wait("gpsimd", st, attn_half_st)
                    p.op("gpsimd", lambda g: g.collective_compute(
                        "AllGather", ALU.bypass, replica_groups=groups,
                        ins=[attn_self.ap()[0:512, :].opt()],
                        outs=[attn_gat.ap()[0:2048, :].opt()]), cc, 1)
            attn_done_st = st.n
        esAB.close()

        p.wait("gpsimd", st, attn_done_st)
        p.op("gpsimd", lambda g: g.collective_compute(
            "AllGather", ALU.bypass, replica_groups=groups,
            ins=[attn_self.ap()[512:1024, :].opt()],
            outs=[attn_gat.ap()[2048:4096, :].opt()]), cc, 1)
        cc_done = cc.n

        with ExitStack() as esC:
            sbC = lambda n_, sh, dt: esC.enter_context(nc.sbuf_tensor(n_, sh, dt))
            htok_all = sbC("htok_all", [128, 16, D], BF16)
            wo_sb = sbC("wo_sb", [128, NKT, D], BF16)
            ag_sb = sbC("ag_sb", [128, NKT, 512], BF16)
            xo_sb = [sbC(f"xo_sb{i}", [128, D], F32) for i in range(2)]
            hp_sb = sbC("hp_sb", [128, D], F32)
            sq_scr = sbC("sq_scr", [128, D], BF16)
            hTt_sb = sbC("hTt_sb", [128, NKT, 128], BF16)
            sum1 = sbC("sum1", [128, 1], F32)
            ssq = sbC("ssq", [128, 1], F32)
            negmean = sbC("negmean", [128, 1], F32)
            m2t = sbC("m2t", [128, 1], F32)
            var = sbC("var", [128, 1], F32)
            srt = sbC("srt", [128, 1], F32)
            rstd = sbC("rstd", [128, 1], F32)
            lmax = sbC("lmax", [128, 1], F32)
            l2 = sbC("l2", [128, E], F32)
            lmax2 = sbC("lmax2", [128, 1], F32)
            msk1 = sbC("msk1", [128, E], F32)
            msk2 = sbC("msk2", [128, E], F32)
            w1g = sbC("w1g", [128, 1], F32)
            w2g = sbC("w2g", [128, 1], F32)
            selb_sb = sbC("selb_sb", [128, 16, E], BF16)
            pfx_sb = sbC("pfx_sb", [128, 16, E], F32)
            tot_sb = sbC("tot_sb", [1, 128], F32)
            toff_sb = sbC("toff_sb", [1, 128], F32)
            btoff_sb = sbC("btoff_sb", [128, 16, E], F32)
            slotf_sb = sbC("slotf_sb", [128, E], F32)
            inv_sb = sbC("inv_sb", [128, E], F32)
            ovr_sb = sbC("ovr_sb", [128, E], F32)
            m1_all = sbC("m1_all", [128, 16, E], BF16)
            slotr_sb = sbC("slotr_sb", [128, E], F32)
            msk2t = sbC("msk2t", [128, E], F32)
            sl_sb = sbC("sl_sb", [128, 1], F32)
            ps_wo = esC.enter_context(nc.psum_tensor("ps_wo", [128, D], F32))
            psTc = esC.enter_context(nc.psum_tensor("psTc", [128, 512], BF16))
            ps_r = esC.enter_context(nc.psum_tensor("ps_r", [128, E], F32))
            ps_p = esC.enter_context(nc.psum_tensor("ps_p", [128, E], F32))
            ps_t1 = esC.enter_context(nc.psum_tensor("ps_t1", [1, E], F32))

            for i4 in range(4):
                p.op("gpsimd", lambda g, i4=i4: g.dma_start(
                    out=wo_sb[:, i4 * 4:(i4 + 1) * 4, :],
                    in_=woT[i4 * 512:(i4 + 1) * 512, :].rearrange("(k p) n -> p k n", p=128)),
                    gld, 16)
            wo_ld = gld.n

            entryC_act, entryC_dve = act.n, dve.n
            bregs = {}
            p.op("gpsimd", lambda g: bregs.__setitem__("scat", g.to_reg(NSL - 1)))
            p.op("gpsimd", lambda g: bregs.__setitem__("gat", g.to_reg(NSL * D - 1)))
            ag_ld = 0
            ag_user = 0
            hp_rd = 0
            wo_add = 0
            psTc_rd = 0
            htok_rd_pe = 0
            gate_done = 0
            xo_ld = [0, 0]
            xo_user = [0, 0]
            firstC = True
            ps_p_rd = 0
            ps_t1_rd = 0
            h_st = 0
            scat_done = [0, 0]
            slotC_done = 0

            for c4 in range(4):
                p.wait("gpsimd", cc, cc_done)
                p.wait("gpsimd", pe, ag_user)
                for kt in range(NKT):
                    p.op("gpsimd", lambda g, kt=kt, c4=c4: g.indirect_dma_start(
                        ag_sb[:, kt, :], None,
                        attn_gat.ap()[0:128, 0:512],
                        IndirectOffsetOnAxis(agidx_sb[:, kt:kt + 1], 1),
                        element_offset=c4 * 512), gld, 16)
                ag_ld = gld.n
                for tsub in range(4):
                    tt = c4 * 4 + tsub
                    xb = tt % 2
                    p.wait("sync", cc, cc_done)
                    p.wait("sync", dve, xo_user[xb])
                    xo_ld[xb] = p.op("sync", lambda sy, tt=tt, b=xb: sy.dma_start(
                        out=xo_sb[b][:], in_=x_own[tt * 128:(tt + 1) * 128, :]), sld, 16)
                    p.wait("tensor", gld, max(ag_ld, wo_ld))
                    if firstC:
                        p.wait("tensor", act, entryC_act)
                        p.wait("tensor", dve, entryC_dve)
                        firstC = False
                    p.wait("tensor", dve, wo_add)
                    for n4 in range(4):
                        for kt in range(NKT):
                            p.op("tensor", lambda t, n4=n4, kt=kt, ts=tsub: t.matmul(
                                ps_wo[:, n4 * 512:(n4 + 1) * 512],
                                ag_sb[:, kt, ts * 128:(ts + 1) * 128],
                                wo_sb[:, kt, n4 * 512:(n4 + 1) * 512],
                                start=(kt == 0), stop=(kt == NKT - 1)),
                                pe if (n4 == 3 and kt == NKT - 1) else None)
                    wo_done = pe.n
                    p.wait("vector", pe, wo_done)
                    p.wait("vector", sld, xo_ld[xb])
                    p.wait("vector", act, hp_rd)
                    p.op("vector", lambda v, b=xb: v.tensor_add(hp_sb[:], xo_sb[b][:], ps_wo[:]), dve)
                    a1 = dve.n
                    wo_add = a1
                    xo_user[xb] = a1
                    p.op("scalar", lambda s_, v=a1: s_.wait_ge(dve.h, v))
                    p.op("scalar", lambda s_: s_.activation(
                        sq_scr[:], hp_sb[:], AF.Square, accum_out=ssq[:]), act)
                    hp_rd = act.n
                    sq_done = act.n
                    p.wait("vector", dve, a1)
                    p.op("vector", lambda v: v.reduce_sum(sum1[:], hp_sb[:], axis=AX.X), dve)
                    a2 = dve.n
                    p.wait("vector", dve, a2)
                    p.op("vector", lambda v: v.tensor_scalar(
                        negmean[:], sum1[:], -1.0 / D, None, ALU.mult), dve)
                    a3 = dve.n
                    p.wait("vector", dve, a3)
                    p.op("vector", lambda v: v.tensor_mul(m2t[:], negmean[:], negmean[:]))
                    p.wait("vector", act, sq_done)
                    p.op("vector", lambda v: v.tensor_scalar(
                        var[:], ssq[:], 1.0 / D, None, ALU.mult), dve)
                    a4 = dve.n
                    p.wait("vector", dve, a4)
                    p.op("vector", lambda v: v.tensor_sub(var[:], var[:], m2t[:]), dve)
                    a5 = dve.n
                    p.op("scalar", lambda s_, v=a5: s_.wait_ge(dve.h, v))
                    p.op("scalar", lambda s_: s_.activation(
                        srt[:], var[:], AF.Sqrt, bias=eps_sb[:]), act)
                    a6 = act.n
                    p.wait("vector", act, a6)
                    p.op("vector", lambda v: v.reciprocal(rstd[:], srt[:]), dve)
                    a7 = dve.n
                    p.wait("vector", dve, a7)
                    p.op("vector", lambda v, tt=tt: v.tensor_scalar(
                        htok_all[:, tt, :], hp_sb[:], negmean[:], rstd[:],
                        ALU.add, ALU.mult), dve)
                    htok_done = dve.n
                    p.wait("scalar", dve, htok_done)
                    h_st = p.op("scalar", lambda s_, tt=tt: s_.dma_start(
                        out=h_tok_dram.ap()[tt * 128:(tt + 1) * 128, :],
                        in_=htok_all[:, tt, :]), st, 16)
                    for ktg in range(4):
                        p.wait("tensor", dve, max(htok_done, psTc_rd))
                        p.wait("tensor", pe, htok_rd_pe)
                        for i in range(4):
                            p.op("tensor", lambda t, ktg=ktg, i=i, tt=tt: t.transpose(
                                psTc[:, i * 128:(i + 1) * 128],
                                htok_all[:, tt, (ktg * 4 + i) * 128:(ktg * 4 + i + 1) * 128],
                                ident_sb[:]), pe if i == 3 else None)
                        trd = pe.n
                        p.wait("vector", pe, trd)
                        dstH = hTt_sb[:, ktg * 4:(ktg + 1) * 4, :]
                        srcH = psTc[:].rearrange("p (k x) -> p k x", k=4)
                        p.op("vector", lambda v, d=dstH, s_=srcH: v.tensor_copy(d, s_), dve)
                        psTc_rd = dve.n
                    hT_done = dve.n
                    p.wait("tensor", dve, max(hT_done, gate_done))
                    for kt in range(NKT):
                        p.op("tensor", lambda t, kt=kt: t.matmul(
                            ps_r[:], hTt_sb[:, kt, :],
                            routerT_sb[:, kt, :], start=(kt == 0), stop=(kt == NKT - 1)),
                            pe if kt == NKT - 1 else None)
                    r_done = pe.n
                    htok_rd_pe = 0
                    p.wait("vector", pe, r_done)
                    p.op("vector", lambda v: v.reduce_max(lmax[:], ps_r[:], axis=AX.X), dve)
                    b1 = dve.n
                    p.wait("vector", dve, b1)
                    p.op("vector", lambda v: v.tensor_scalar(
                        msk1[:], ps_r[:], lmax[:], None, ALU.is_ge), dve)
                    b2 = dve.n
                    p.wait("vector", dve, b2)
                    p.op("vector", lambda v, tt=tt: v.tensor_copy(m1_all[:, tt, :], msk1[:]))
                    p.op("vector", lambda v: v.scalar_tensor_tensor(
                        l2[:], msk1[:], -1e30, ps_r[:], ALU.mult, ALU.add), dve)
                    b3 = dve.n
                    p.wait("vector", dve, b3)
                    p.op("vector", lambda v: v.reduce_max(lmax2[:], l2[:], axis=AX.X), dve)
                    b4 = dve.n
                    p.wait("vector", dve, b4)
                    p.op("vector", lambda v: v.tensor_scalar(
                        msk2[:], l2[:], lmax2[:], None, ALU.is_ge), dve)
                    b5 = dve.n
                    p.op("scalar", lambda s_, v=b5: s_.wait_ge(dve.h, v))
                    p.op("scalar", lambda s_: s_.activation(
                        w1g[:], lmax2[:], AF.Sigmoid, bias=lmax[:], scale=-1.0), act)
                    b6 = act.n
                    p.wait("vector", act, b6)
                    p.op("vector", lambda v, tt=tt: v.tensor_scalar(
                        g_sb[:, tt, :], msk1[:], w1g[:], None, ALU.mult))
                    p.op("vector", lambda v: v.tensor_scalar(
                        w2g[:], w1g[:], -1.0, 1.0, ALU.mult, ALU.add), dve)
                    b7 = dve.n
                    p.wait("vector", dve, b7)
                    p.op("vector", lambda v, tt=tt: v.scalar_tensor_tensor(
                        g_sb[:, tt, :], msk2[:], w2g[:], g_sb[:, tt, :],
                        ALU.mult, ALU.add))
                    p.op("vector", lambda v, tt=tt: v.tensor_add(
                        selb_sb[:, tt, :], msk1[:], msk2[:]), dve)
                    gate_done = dve.n
                    p.wait("tensor", dve, max(gate_done, ps_p_rd, ps_t1_rd))
                    p.op("tensor", lambda t, tt=tt: t.matmul(
                        ps_p[:], ltc_sb[:], selb_sb[:, tt, :], start=True, stop=True))
                    p.op("tensor", lambda t, tt=tt: t.matmul(
                        ps_t1[:], ones_col[:], selb_sb[:, tt, :], start=True, stop=True),
                        pe)
                    pfx_done = pe.n
                    p.wait("vector", pe, pfx_done)
                    p.op("vector", lambda v, tt=tt: v.tensor_copy(pfx_sb[:, tt, :], ps_p[:]))
                    p.op("vector", lambda v, tt=tt: v.tensor_copy(
                        tot_sb[0:1, tt * 8:(tt + 1) * 8], ps_t1[0:1, :]), dve)
                    ps_p_rd = dve.n
                    ps_t1_rd = dve.n

                    if tt in (7, 15):
                        hh2 = tt // 8
                        base_col = hh2 * 64
                        p.wait("vector", dve, dve.n)
                        p.op("vector", lambda v, bc=base_col: v.tensor_scalar(
                            toff_sb[0:1, bc:bc + 8], tot_sb[0:1, bc:bc + 8],
                            0.0, None, ALU.mult), dve)
                        for k in range(1, 8):
                            p.wait("vector", dve, dve.n)
                            p.op("vector", lambda v, bc=base_col, k=k: v.tensor_add(
                                toff_sb[0:1, bc + k * 8:bc + (k + 1) * 8],
                                toff_sb[0:1, bc + (k - 1) * 8:bc + k * 8],
                                tot_sb[0:1, bc + (k - 1) * 8:bc + k * 8]), dve)
                        p.wait("vector", dve, dve.n)
                        p.op("vector", lambda v, bc=base_col: v.tensor_add(
                            toff_sb[0:1, bc:bc + 64],
                            toff_sb[0:1, bc:bc + 64],
                            sbase_sb[0:1, bc:bc + 64]), dve)
                        toff_done = dve.n
                        p.wait("tensor", dve, max(toff_done, wo_add))
                        p.op("tensor", lambda t, bc=base_col: t.matmul(
                            ps_wo[:, 0:64], ones_row[:],
                            toff_sb[0:1, bc:bc + 64], start=True, stop=True), pe)
                        bc_done2 = pe.n
                        p.wait("vector", pe, bc_done2)
                        p.op("vector", lambda v, h2=hh2: v.tensor_copy(
                            btoff_sb[:, h2 * 8:(h2 + 1) * 8, :],
                            ps_wo[:, 0:64].rearrange("p (a b) -> p a b", b=8)), dve)
                        btoff_done = dve.n
                        wo_add = dve.n
                        for t2 in range(hh2 * 8, hh2 * 8 + 8):
                            p.wait("vector", dve, dve.n)
                            p.op("vector", lambda v, t2=t2: v.tensor_add(
                                slotf_sb[:], pfx_sb[:, t2, :], btoff_sb[:, t2, :]))
                            p.op("vector", lambda v, t2=t2: v.tensor_scalar(
                                ovr_sb[:], pfx_sb[:, t2, :], CAP + 0.5, None, ALU.is_gt), dve)
                            p.wait("vector", dve, dve.n)
                            p.op("vector", lambda v: v.tensor_scalar(
                                slotf_sb[:], slotf_sb[:], -1.0, None, ALU.add), dve)
                            p.wait("vector", dve, dve.n)
                            p.op("vector", lambda v: v.scalar_tensor_tensor(
                                slotr_sb[:], ovr_sb[:], 1e6, slotf_sb[:], ALU.mult, ALU.add), dve)
                            p.wait("vector", dve, dve.n)
                            p.op("vector", lambda v, t2=t2: v.tensor_scalar(
                                inv_sb[:], selb_sb[:, t2, :], -1.0, 1.0, ALU.mult, ALU.add), dve)
                            p.wait("vector", dve, dve.n)
                            p.op("vector", lambda v: v.scalar_tensor_tensor(
                                slotf_sb[:], inv_sb[:], 1e6, slotr_sb[:], ALU.mult, ALU.add), dve)
                            p.wait("vector", dve, dve.n)
                            p.op("vector", lambda v: v.tensor_scalar(
                                slotf_sb[:], slotf_sb[:], float(D), None, ALU.mult), dve)
                            p.wait("vector", dve, dve.n)
                            p.op("vector", lambda v, t2=t2: v.tensor_copy(
                                slot_sb[:, t2, :], slotf_sb[:]))
                            p.op("vector", lambda v, t2=t2: v.tensor_mul(
                                inv_sb[:], m1_all[:, t2, :], slotr_sb[:]), dve)
                            p.wait("vector", dve, dve.n)
                            p.op("vector", lambda v: v.reduce_sum(sl_sb[:], inv_sb[:], axis=AX.X), dve)
                            p.wait("vector", dve, dve.n)
                            p.op("vector", lambda v, t2=t2: v.tensor_copy(
                                slot12_sb[:, t2, 0:1], sl_sb[:]))
                            p.op("vector", lambda v, t2=t2: v.tensor_sub(
                                msk2t[:], selb_sb[:, t2, :], m1_all[:, t2, :]), dve)
                            p.wait("vector", dve, dve.n)
                            p.op("vector", lambda v: v.tensor_mul(inv_sb[:], msk2t[:], slotr_sb[:]), dve)
                            p.wait("vector", dve, dve.n)
                            p.op("vector", lambda v: v.reduce_sum(sl_sb[:], inv_sb[:], axis=AX.X), dve)
                            p.wait("vector", dve, dve.n)
                            p.op("vector", lambda v, t2=t2: v.tensor_copy(
                                slot12_sb[:, t2, 1:2], sl_sb[:]), dve)
                            slotC_done = dve.n
                        p.wait("gpsimd", dve, slotC_done)
                        for t2 in range(hh2 * 8, hh2 * 8 + 8):
                            for k2 in range(2):
                                p.op("gpsimd", lambda g, t2=t2, k2=k2: g.indirect_dma_start(
                                    xs_dram.ap()[0:128, :],
                                    IndirectOffsetOnAxis(slot12_sb[:, t2, k2:k2 + 1], 0),
                                    htok_all[:, t2, :], None,
                                    bounds_check=bregs["scat"], oob_is_err=False), gld, 16)
                        scat_done[hh2] = gld.n
                ag_user = pe.n
            phaseC_pe = pe.n
            phaseC_act = act.n
            phaseC_dve = dve.n

        y_acc = sb("y_acc", [128, 16, D], BF16)
        with ExitStack() as esD:
            sbD = lambda n_, sh, dt: esD.enter_context(nc.sbuf_tensor(n_, sh, dt))
            w1c = [sbD(f"w1c{i}", [128, NKT, 256], BF16) for i in range(2)]
            w2e = sbD("w2e", [128, NCI, D], BF16)
            xst = [sbD(f"xst{i}", [128, D], BF16) for i in range(2)]
            xsT = sbD("xsT", [128, NKT, 6 * 128], BF16)
            yact = sbD("yact", [128, NCI, 6 * 128], BF16)
            sact = sbD("sact", [128, 6 * 128], BF16)
            ysb = sbD("ysb", [128, D], BF16)
            ye = [sbD(f"ye{i}", [128, D], BF16) for i in range(2)]
            ps_g = esD.enter_context(nc.psum_tensor("ps_g", [128, 6 * 128], F32))
            ps_u = esD.enter_context(nc.psum_tensor("ps_u", [128, 6 * 128], F32))
            ps_y = [esD.enter_context(nc.psum_tensor(f"ps_y{i}", [128, 512], F32))
                    for i in range(2)]
            psTd = esD.enter_context(nc.psum_tensor("psTd", [128, 512], BF16))

            entryD_act, entryD_dve, entryD_pe = act.n, dve.n, pe.n
            p.wait("gpsimd", gld, scat_done[1])
            p.wait("gpsimd", pe, pe.n)
            p.wait("gpsimd", act, act.n)
            p.wait("gpsimd", dve, dve.n)
            p.wait("gpsimd", st, st.n)
            p.op("vector", lambda v: v.memset(ye[0][:], 0.0))
            p.op("vector", lambda v: v.memset(ye[1][:], 0.0), dve)
            ye_zero = dve.n

            w1_ld = [0, 0]
            w1_user = [pe.n, pe.n]
            w2_user = pe.n
            w2_ld = 0
            xst_ld = [0, 0]
            xst_user = [pe.n, pe.n]
            xsT_user = pe.n
            psTd_rd = dve.n
            ps_g_rd = act.n
            ps_u_rd = dve.n
            ps_y_rd = [act.n, act.n]
            ysb_st = 0
            ye_user = [ye_zero, ye_zero]
            firstD = True

            def load_w1(e, ci):
                wb = ci % 2
                p.wait("gpsimd", pe, w1_user[wb])
                p.op("gpsimd", lambda g, e=e, ci=ci, b=wb: g.dma_start(
                    out=w1c[b][:], in_=w1P[e, ci]), gld, 16)
                w1_ld[wb] = gld.n

            def load_w2(e):
                for ci in range(NCI):
                    if ci == 0:
                        p.wait("gpsimd", pe, w2_user)
                    p.op("gpsimd", lambda g, e=e, ci=ci: g.dma_start(
                        out=w2e[:, ci, :],
                        in_=w2T[e, ci * 128:(ci + 1) * 128, :]), gld, 16)
                return gld.n

            for e in range(E):
                if e == 0:
                    w2_ld = load_w2(0)
                    load_w1(e, 0)
                    load_w1(e, 1)
                else:
                    w2_ld = w2_ld_next
                for g6 in range(6):
                    hh2, rt = g6 // 3, g6 % 3
                    xb2 = g6 % 2
                    row0 = hh2 * (E * CAP) + e * CAP + rt * 128
                    p.wait("sync", gld, scat_done[hh2])
                    p.wait("sync", pe, xst_user[xb2])
                    xst_ld[xb2] = p.op("sync", lambda sy, r0=row0, b=xb2: sy.dma_start(
                        out=xst[b][:], in_=xs_dram.ap()[r0:r0 + 128, :]), sld, 16)
                    p.wait("tensor", sld, xst_ld[xb2])
                    if firstD:
                        p.wait("tensor", act, entryD_act)
                        p.wait("tensor", dve, entryD_dve)
                        firstD = False
                    p.wait("tensor", pe, xsT_user)
                    for ktg in range(4):
                        p.wait("tensor", dve, psTd_rd)
                        for i in range(4):
                            p.op("tensor", lambda t, ktg=ktg, i=i, b=xb2: t.transpose(
                                psTd[:, i * 128:(i + 1) * 128],
                                xst[b][:, (ktg * 4 + i) * 128:(ktg * 4 + i + 1) * 128],
                                ident_sb[:]), pe if i == 3 else None)
                        trd = pe.n
                        if ktg == 3:
                            xst_user[xb2] = trd
                        p.wait("vector", pe, trd)
                        dstX = xsT[:, ktg * 4:(ktg + 1) * 4, g6 * 128:(g6 + 1) * 128]
                        srcX = psTd[:].rearrange("p (k x) -> p k x", k=4)
                        p.op("vector", lambda v, d=dstX, s_=srcX: v.tensor_copy(d, s_), dve)
                        psTd_rd = dve.n
                xsT_done = dve.n
                if dbg and e == 0:
                    p.wait("sync", dve, xsT_done)
                    xsT_st = p.op("sync", lambda sy: sy.dma_start(out=dbg_xsT, in_=xsT[:]), st, 16)
                    p.wait("vector", st, xsT_st)
                    p.wait("tensor", st, xsT_st)
                for ci in range(NCI):
                    wb = ci % 2
                    p.wait("tensor", gld, w1_ld[wb])
                    p.wait("tensor", dve, xsT_done)
                    p.wait("tensor", act, ps_g_rd)
                    for (o_, n_) in ((0, 512), (512, 256)):
                        for kt in range(NKT):
                            p.op("tensor", lambda t, b=wb, kt=kt, o_=o_, n_=n_: t.matmul(
                                ps_g[:, o_:o_ + n_],
                                w1c[b][:, kt, 0:128],
                                xsT[:, kt, o_:o_ + n_],
                                start=(kt == 0), stop=(kt == NKT - 1)),
                                pe if (o_ == 512 and kt == NKT - 1) else None)
                    g_done = pe.n
                    p.wait("tensor", dve, ps_u_rd)
                    for (o_, n_) in ((0, 512), (512, 256)):
                        for kt in range(NKT):
                            p.op("tensor", lambda t, b=wb, kt=kt, o_=o_, n_=n_: t.matmul(
                                ps_u[:, o_:o_ + n_],
                                w1c[b][:, kt, 128:256],
                                xsT[:, kt, o_:o_ + n_],
                                start=(kt == 0), stop=(kt == NKT - 1)),
                                pe if (o_ == 512 and kt == NKT - 1) else None)
                    u_done = pe.n
                    w1_user[wb] = pe.n
                    if ci + 2 < NCI:
                        load_w1(e, ci + 2)
                    elif e + 1 < E and ci + 2 < NCI + 2:
                        load_w1(e + 1, (ci + 2) % 2)
                    p.op("scalar", lambda s_, v=g_done: s_.wait_ge(pe.h, v))
                    p.op("scalar", lambda s_, v=ps_u_rd: s_.wait_ge(dve.h, v))
                    p.op("scalar", lambda s_: s_.activation(sact[:], ps_g[:], AF.Silu), act)
                    ps_g_rd = act.n
                    silu_done = act.n
                    p.wait("vector", pe, u_done)
                    p.wait("vector", act, silu_done)
                    p.op("vector", lambda v, ci=ci: v.tensor_mul(
                        yact[:, ci, :], sact[:], ps_u[:]), dve)
                    ps_u_rd = dve.n
                yact_done = dve.n
                if dbg and e == 0:
                    p.wait("sync", dve, yact_done)
                    ya2_st = p.op("sync", lambda sy: sy.dma_start(out=dbg_ya2, in_=yact[:]), st, 16)
                    p.wait("tensor", st, ya2_st)
                    p.wait("vector", st, ya2_st)
                xsT_user = pe.n
                for g6 in range(6):
                    hh2, rt = g6 // 3, g6 % 3
                    row0 = hh2 * (E * CAP) + e * CAP + rt * 128
                    for n4 in range(4):
                        yb = n4 % 2
                        p.wait("tensor", gld, w2_ld)
                        p.wait("tensor", dve, yact_done)
                        p.wait("tensor", act, ps_y_rd[yb])
                        for ci in range(NCI):
                            p.op("tensor", lambda t, ci=ci, g6=g6, n4=n4, b=yb: t.matmul(
                                ps_y[b][:],
                                yact[:, ci, g6 * 128:(g6 + 1) * 128],
                                w2e[:, ci, n4 * 512:(n4 + 1) * 512],
                                start=(ci == 0), stop=(ci == NCI - 1)),
                                pe if ci == NCI - 1 else None)
                        y2_done = pe.n
                        p.op("scalar", lambda s_, v=y2_done: s_.wait_ge(pe.h, v))
                        if n4 <= 1:
                            p.op("scalar", lambda s_, v=ysb_st: s_.wait_ge(st.h, v))
                        p.op("scalar", lambda s_, n4=n4, b=yb: s_.copy(
                            ysb[:, n4 * 512:(n4 + 1) * 512], ps_y[b][:]), act)
                        ps_y_rd[yb] = act.n
                    ys_evac = act.n
                    p.op("scalar", lambda s_, v=ys_evac: s_.wait_ge(act.h, v))
                    ysb_st = p.op("scalar", lambda s_, r0=row0: s_.dma_start(
                        out=ys_dram.ap()[r0:r0 + 128, :], in_=ysb[:]), st, 16)
                w2_user = pe.n
                ys_done = ysb_st
                p.wait("gpsimd", st, ys_done)
                for tt in range(16):
                    gb = tt % 2
                    p.wait("gpsimd", dve, ye_user[gb])
                    gat = p.op("gpsimd", lambda g, tt=tt, e=e, b=gb: g.indirect_dma_start(
                        ye[b][:], None,
                        ys_dram.ap()[0:128, :],
                        IndirectOffsetOnAxis(slot_sb[:, tt, e:e + 1], 1),
                        bounds_check=bregs["gat"], oob_is_err=False), gld, 16)
                    p.wait("vector", gld, gat)
                    gap = g_sb[:, tt, e:e + 1]
                    if e == 0:
                        p.op("vector", lambda v, tt=tt, b=gb, gap=gap: v.tensor_scalar(
                            y_acc[:, tt, :], ye[b][:], gap, None, ALU.mult), dve)
                    else:
                        p.op("vector", lambda v, tt=tt, b=gb, gap=gap: v.scalar_tensor_tensor(
                            y_acc[:, tt, :], ye[b][:], gap, y_acc[:, tt, :],
                            ALU.mult, ALU.add), dve)
                    ye_user[gb] = dve.n
                if e + 1 < E:
                    w2_ld_next = load_w2(e + 1)
            y_done = dve.n
            lastD_act = act.n

        with ExitStack() as esE:
            sbE = lambda n_, sh, dt: esE.enter_context(nc.sbuf_tensor(n_, sh, dt))
            hE = [sbE(f"hE{i}", [128, D], BF16) for i in range(2)]
            hp2 = sbE("hp2", [128, D], F32)
            outt = [sbE(f"outt{i}", [128, D], F32) for i in range(2)]
            sqE = sbE("sqE", [128, D], BF16)
            sum1e = sbE("sum1e", [128, 1], F32)
            ssqe = sbE("ssqe", [128, 1], F32)
            negme = sbE("negme", [128, 1], F32)
            m2te = sbE("m2te", [128, 1], F32)
            vare = sbE("vare", [128, 1], F32)
            srte = sbE("srte", [128, 1], F32)
            rstde = sbE("rstde", [128, 1], F32)

            hE_ld = [0, 0]
            hE_user = [0, 0]
            hp2_sq = act.n
            out_user = [st.n, st.n]
            p.wait("sync", pe, pe.n)
            p.wait("sync", dve, y_done)
            p.wait("sync", act, lastD_act)
            p.wait("sync", gld, gld.n)
            for tt in range(16):
                ob = tt % 2
                p.wait("sync", st, h_st)
                p.wait("sync", dve, hE_user[ob])
                hE_ld[ob] = p.op("sync", lambda sy, tt=tt, b=ob: sy.dma_start(
                    out=hE[b][:], in_=h_tok_dram.ap()[tt * 128:(tt + 1) * 128, :]), sld, 16)
                p.wait("vector", sld, hE_ld[ob])
                p.wait("vector", dve, y_done)
                p.wait("vector", act, hp2_sq)
                p.op("vector", lambda v, tt=tt, b=ob: v.tensor_add(
                    hp2[:], hE[b][:], y_acc[:, tt, :]), dve)
                c1_ = dve.n
                hE_user[ob] = c1_
                p.op("scalar", lambda s_, v=c1_: s_.wait_ge(dve.h, v))
                p.op("scalar", lambda s_: s_.activation(
                    sqE[:], hp2[:], AF.Square, accum_out=ssqe[:]), act)
                hp2_sq = act.n
                sqe_done = act.n
                p.wait("vector", dve, c1_)
                p.op("vector", lambda v: v.reduce_sum(sum1e[:], hp2[:], axis=AX.X), dve)
                c2_ = dve.n
                p.wait("vector", dve, c2_)
                p.op("vector", lambda v: v.tensor_scalar(
                    negme[:], sum1e[:], -1.0 / D, None, ALU.mult), dve)
                c3_ = dve.n
                p.wait("vector", dve, c3_)
                p.op("vector", lambda v: v.tensor_mul(m2te[:], negme[:], negme[:]))
                p.wait("vector", act, sqe_done)
                p.op("vector", lambda v: v.tensor_scalar(
                    vare[:], ssqe[:], 1.0 / D, None, ALU.mult), dve)
                c4_ = dve.n
                p.wait("vector", dve, c4_)
                p.op("vector", lambda v: v.tensor_sub(vare[:], vare[:], m2te[:]), dve)
                c5_ = dve.n
                p.op("scalar", lambda s_, v=c5_: s_.wait_ge(dve.h, v))
                p.op("scalar", lambda s_: s_.activation(
                    srte[:], vare[:], AF.Sqrt, bias=eps_sb[:]), act)
                c6_ = act.n
                p.wait("vector", act, c6_)
                p.op("vector", lambda v: v.reciprocal(rstde[:], srte[:]), dve)
                c7_ = dve.n
                p.wait("vector", dve, c7_)
                p.wait("vector", st, out_user[ob])
                p.op("vector", lambda v, b=ob: v.tensor_scalar(
                    outt[b][:], hp2[:], negme[:], rstde[:], ALU.add, ALU.mult), dve)
                c8_ = dve.n
                p.wait("scalar", dve, c8_)
                p.op("scalar", lambda s_, tt=tt, b=ob: s_.dma_start(
                    out=out[tt * 128:(tt + 1) * 128, :], in_=outt[b][:]), st, 16)
                out_user[ob] = st.n
        out_st_total = st.n

        if dbg:
            p.wait("sync", cc, cc_done)
            p.wait("sync", st, out_st_total)
            p.op("sync", lambda sy: sy.dma_start(out=dbg_attn, in_=attn_gat.ap()), st, 16)
            p.op("sync", lambda sy: sy.dma_start(
                out=dbg_h.rearrange("t d -> t d"), in_=h_tok_dram.ap()), st, 16)
            p.op("sync", lambda sy: sy.dma_start(out=dbg_g, in_=g_sb[:]), st, 16)
            p.op("sync", lambda sy: sy.dma_start(out=dbg_s12, in_=slot12_sb[:]), st, 16)
            p.op("sync", lambda sy: sy.dma_start(out=dbg_slot, in_=slot_sb[:]), st, 16)
            p.op("sync", lambda sy: sy.dma_start(out=dbg_xs, in_=xs_dram.ap()), st, 16)
            p.op("sync", lambda sy: sy.dma_start(out=dbg_ys, in_=ys_dram.ap()), st, 16)
            p.op("sync", lambda sy: sy.dma_start(out=dbg_yacc, in_=y_acc[:]), st, 16)

        p.wait("sync", st, st.n)

        block = es.enter_context(nc.Block())

        @block.tensor
        def _(t):
            for fn in p.streams["tensor"]:
                fn(t)

        @block.vector
        def _(v):
            for fn in p.streams["vector"]:
                fn(v)

        @block.scalar
        def _(s_):
            for fn in p.streams["scalar"]:
                fn(s_)

        @block.gpsimd
        def _(g):
            for fn in p.streams["gpsimd"]:
                fn(g)

        @block.sync
        def _(sy):
            for fn in p.streams["sync"]:
                fn(sy)

    nc._plan_streams = {k: list(v) for k, v in p.streams.items()}
    return nc


_ROPE_PERM = np.concatenate([np.arange(0, 128, 2), np.arange(1, 128, 2)])


def _host_prepare(x, Wq, Wk, Wv, Wo, router_w, w1, w2):
    x = np.asarray(x, dtype=np.float32)
    xT = [np.ascontiguousarray(x[b].T) for b in range(2)]
    wq_s, wk_s, wv_s = [], [], []
    for hg in range(4):
        def permslice(W):
            blk = np.asarray(W)[hg * 512:(hg + 1) * 512].reshape(4, 128, D)[:, _ROPE_PERM, :]
            return np.ascontiguousarray(blk.reshape(512, D).T)
        wq_s.append(permslice(Wq))
        wk_s.append(permslice(Wk))
        wv_s.append(np.ascontiguousarray(
            np.asarray(Wv)[hg * 512:(hg + 1) * 512].T))
    woT = np.ascontiguousarray(np.asarray(Wo).T)
    routerT = np.ascontiguousarray(np.asarray(router_w).T)
    _A = np.asarray(w1).reshape(E, 22, 128, NKT, 128).transpose(0, 1, 4, 3, 2)
    w1P = np.ascontiguousarray(
        np.stack([_A[:, :NCI], _A[:, NCI:]], axis=4)).reshape(E, NCI, 128, NKT, 256)
    w2T = np.ascontiguousarray(np.asarray(w2).transpose(0, 2, 1))
    inv_freq = (1.0 / (10000.0 ** (np.arange(0, 128, 2, dtype=np.float32) / 128.0))).astype(np.float32)
    t = np.arange(S, dtype=np.float32)
    freqs = np.outer(t, inv_freq).astype(np.float32)
    cosT = np.cos(freqs).astype(np.float32)
    sinT = np.sin(freqs).astype(np.float32)
    ident = np.eye(128, dtype=np.float32)
    kk, qq = np.meshgrid(np.arange(128), np.arange(128), indexing="ij")
    diag = (kk <= qq).astype(np.float32)
    pp = np.arange(128, dtype=np.int64)
    ii, jj = np.meshgrid(np.arange(128), np.arange(128), indexing="ij")
    ltc = (jj >= ii).astype(np.float32)
    sbase = np.zeros((1, 128), dtype=np.float32)
    for tile in range(16):
        for e in range(8):
            sbase[0, tile * 8 + e] = (tile // 8) * (8 * 384) + e * 384
    in_maps = []
    for c in range(8):
        b, hg = c // 4, c % 4
        tok_off = 2048 * hg
        half = tok_off // 4096
        agidx = np.stack(
            [(half * 2048 + kt * 128 + pp) * 4096 + (tok_off % 4096)
             for kt in range(NKT)], axis=1).astype(np.int32)
        in_maps.append({
            "xT": xT[b],
            "x_own": np.ascontiguousarray(x[b, tok_off:tok_off + TOK]),
            "wqT": wq_s[hg], "wkT": wk_s[hg], "wvT": wv_s[hg],
            "woT": woT, "routerT": routerT, "w1P": w1P, "w2T": w2T,
            "cosT": cosT, "sinT": sinT, "identD": ident, "diagD": diag,
            "agidx": agidx, "ltcD": ltc, "sbaseD": sbase,
        })
    return in_maps


_NC_CACHE = {}


def kernel(L0_attn_x, L0_attn_Wq, L0_attn_Wk, L0_attn_Wv, L0_attn_Wo,
           L0_ffn_router_w, L0_ffn_w1, L0_ffn_w2, _dbg=False):
    in_maps = _host_prepare(L0_attn_x, L0_attn_Wq, L0_attn_Wk, L0_attn_Wv,
                            L0_attn_Wo, L0_ffn_router_w, L0_ffn_w1, L0_ffn_w2)
    key = bool(_dbg)
    if key not in _NC_CACHE:
        _NC_CACHE[key] = build(dbg=_dbg)
    nc = _NC_CACHE[key]
    res = run_bass_kernel_spmd(nc, in_maps, core_ids=list(range(8)))
    out = np.empty((2, S, D), dtype=np.float32)
    for c in range(8):
        b, hg = c // 4, c % 4
        out[b, 2048 * hg:2048 * (hg + 1)] = res.results[c]["out"]
    if _dbg:
        return out, res
    return out


# version 09 (final)
# speedup vs baseline: 1.1291x; 1.0822x over previous
import math
from contextlib import ExitStack

import numpy as np
import ml_dtypes
import concourse.bass as bass
import concourse.mybir as mybir
from concourse.ap import AP
from concourse.bass_core_rust import IndirectOffsetOnAxis
from concourse.bass_utils import run_bass_kernel_spmd

F32 = mybir.dt.float32
BF16 = mybir.dt.bfloat16
I32 = mybir.dt.int32
AF = mybir.ActivationFunctionType
ALU = mybir.AluOpType
AX = mybir.AxisListType

S = 8192
D = 2048
DH = 128
NHC = 4
E = 8
IFF = 1408
TOK = 2048
NKT = D // 128
NCI = IFF // 128
CAP = 384
NSL = 2 * E * CAP
RSQ = 1.0 / math.sqrt(128.0)

ENGINES = ("tensor", "vector", "scalar", "gpsimd", "sync")


class Sem:
    __slots__ = ("h", "n")


class Plan:
    def __init__(self, nc, es):
        self.nc = nc
        self.es = es
        self.streams = {k: [] for k in ENGINES}

    def sem(self, name):
        s = Sem()
        s.h = self.es.enter_context(self.nc.semaphore(name))
        s.n = 0
        return s

    def op(self, eng, fn, inc=None, k=1):
        if inc is not None:
            inc.n += k

            def run(EE, fn=fn, h=inc.h, k=k):
                fn(EE).then_inc(h, k)
        else:
            def run(EE, fn=fn):
                fn(EE)
        self.streams[eng].append(run)
        return inc.n if inc is not None else None

    def wait(self, eng, sem, v=None):
        v = sem.n if v is None else v
        if v <= 0:
            return
        self.streams[eng].append(lambda EE, h=sem.h, v=v: EE.wait_ge(h, v))


def build(dbg=False):
    nc = bass.Bass("TRN2", target_bir_lowering=False)

    xT = nc.dram_tensor("xT", [D, S], F32, kind="ExternalInput").ap()
    x_own = nc.dram_tensor("x_own", [TOK, D], F32, kind="ExternalInput").ap()
    wqT = nc.dram_tensor("wqT", [D, NHC * DH], F32, kind="ExternalInput").ap()
    wkT = nc.dram_tensor("wkT", [D, NHC * DH], F32, kind="ExternalInput").ap()
    wvT = nc.dram_tensor("wvT", [D, NHC * DH], F32, kind="ExternalInput").ap()
    woT = nc.dram_tensor("woT", [D, D], F32, kind="ExternalInput").ap()
    routerT = nc.dram_tensor("routerT", [D, E], F32, kind="ExternalInput").ap()
    w1P = nc.dram_tensor("w1P", [E, NCI, 128, NKT, 256], F32, kind="ExternalInput").ap()
    w2T = nc.dram_tensor("w2T", [E, IFF, D], F32, kind="ExternalInput").ap()
    cosT = nc.dram_tensor("cosT", [S, 64], F32, kind="ExternalInput").ap()
    sinT = nc.dram_tensor("sinT", [S, 64], F32, kind="ExternalInput").ap()
    identD = nc.dram_tensor("identD", [128, 128], F32, kind="ExternalInput").ap()
    diagD = nc.dram_tensor("diagD", [128, 128], F32, kind="ExternalInput").ap()
    agidx = nc.dram_tensor("agidx", [128, NKT], I32, kind="ExternalInput").ap()
    ltcD = nc.dram_tensor("ltcD", [128, 128], F32, kind="ExternalInput").ap()
    sbaseD = nc.dram_tensor("sbaseD", [1, 128], F32, kind="ExternalInput").ap()
    iotaD = nc.dram_tensor("iotaD", [128, 1], F32, kind="ExternalInput").ap()

    out = nc.dram_tensor("out", [TOK, D], F32, kind="ExternalOutput").ap()

    attn_self = nc.dram_tensor("attn_self", [2 * NHC * DH, S // 2], BF16)
    attn_gat = nc.dram_tensor("attn_gat", [2 * 16 * DH, S // 2], BF16)
    h_tok_dram = nc.dram_tensor("h_tok_dram", [TOK, D], BF16)
    xs_dram = nc.dram_tensor("xs_dram", [2 * E * 384, D], BF16)
    tg_dram = nc.dram_tensor("tg_dram", [NSL, 2], F32)
    y2tok = nc.dram_tensor("y2tok", [2 * TOK, D], BF16)
    groups = [[0, 1, 2, 3], [4, 5, 6, 7]]

    if dbg:
        dbg_attn = nc.dram_tensor("dbg_attn", [16 * DH, S], BF16, kind="ExternalOutput").ap()
        dbg_h = nc.dram_tensor("dbg_h", [TOK, D], BF16, kind="ExternalOutput").ap()
        dbg_g = nc.dram_tensor("dbg_g", [128, 16, E], F32, kind="ExternalOutput").ap()
        dbg_den = nc.dram_tensor("dbg_den", [32, 1024], F32, kind="ExternalOutput").ap()
        dbg_s12 = nc.dram_tensor("dbg_s12", [128, 16, 2], I32, kind="ExternalOutput").ap()
        dbg_slot = nc.dram_tensor("dbg_slot", [128, 16, E], I32, kind="ExternalOutput").ap()
        dbg_xs = nc.dram_tensor("dbg_xs", [NSL, D], BF16, kind="ExternalOutput").ap()
        dbg_tg = nc.dram_tensor("dbg_tg", [NSL, 2], F32, kind="ExternalOutput").ap()
        dbg_y2t = nc.dram_tensor("dbg_y2t", [2 * TOK, D], BF16, kind="ExternalOutput").ap()
        dbg_xsT = nc.dram_tensor("dbg_xsT", [128, NKT, 768], BF16, kind="ExternalOutput").ap()
        dbg_ya2 = nc.dram_tensor("dbg_ya2", [128, NCI, 768], BF16, kind="ExternalOutput").ap()
        dbg_qT = nc.dram_tensor("dbg_qT", [NHC * 128, 1024], BF16, kind="ExternalOutput").ap()
        dbg_y = nc.dram_tensor("dbg_y", [128, 8, D], BF16, kind="ExternalOutput").ap()
        dbg_ya = nc.dram_tensor("dbg_ya", [128, NCI, 1024], BF16, kind="ExternalOutput").ap()
        dbg_pt = nc.dram_tensor("dbg_pt", [128, 1024], BF16, kind="ExternalOutput").ap()

    es = ExitStack()
    with es:
        p = Plan(nc, es)
        gld = p.sem("gld")
        sld = p.sem("sld")
        pe = p.sem("pe")
        act = p.sem("act")
        dve = p.sem("dve")
        st = p.sem("st")
        cc = p.sem("cc")
        pool = p.sem("pool")
        scats = p.sem("scats")

        sb = lambda name, shape, dt: es.enter_context(nc.sbuf_tensor(name, shape, dt))

        ident_sb = sb("ident_sb", [128, 128], BF16)
        diag_sb = sb("diag_sb", [128, 128], BF16)
        ones_col = sb("ones_col", [128, 1], BF16)
        ones_row = sb("ones_row", [1, 128], F32)
        ones_rowb = sb("ones_rowb", [1, 128], BF16)
        eps_sb = sb("eps_sb", [128, 1], F32)
        routerT_sb = sb("routerT_sb", [128, NKT, E], BF16)
        agidx_sb = sb("agidx_sb", [128, NKT], I32)
        ltc_sb = sb("ltc_sb", [128, 128], BF16)
        sbase_sb = sb("sbase_sb", [1, 128], F32)
        iota_sb = sb("iota_sb", [128, 1], F32)
        vg_sb = sb("vg_sb", [128, 16, 2, 2], F32)
        slot12_sb = sb("slot12_sb", [128, 16, 2], I32)

        p.op("gpsimd", lambda g: g.dma_start(out=ident_sb[:], in_=identD), gld, 16)
        p.op("gpsimd", lambda g: g.dma_start(out=diag_sb[:], in_=diagD), gld, 16)
        p.op("gpsimd", lambda g: g.dma_start(
            out=routerT_sb[:], in_=routerT.rearrange("(k p) e -> p k e", p=128)), gld, 16)
        p.op("gpsimd", lambda g: g.memset(ones_col[:], 1.0))
        p.op("gpsimd", lambda g: g.memset(ones_row[:], 1.0))
        p.op("gpsimd", lambda g: g.memset(ones_rowb[:], 1.0))
        p.op("gpsimd", lambda g: g.memset(eps_sb[:], 1e-5))
        p.op("sync", lambda sy: sy.dma_start(out=agidx_sb[:], in_=agidx), sld, 16)
        p.op("gpsimd", lambda g: g.dma_start(out=ltc_sb[:], in_=ltcD), gld, 16)
        p.op("sync", lambda sy: sy.dma_start(out=sbase_sb[:], in_=sbaseD), sld, 16)
        p.op("sync", lambda sy: sy.dma_start(out=iota_sb[:], in_=iotaD), sld, 16)
        consts_gld = gld.n

        esAB = ExitStack()
        kT_sb = esAB.enter_context(nc.sbuf_tensor("kT_sb", [128, NHC, S], BF16))
        v_sb = esAB.enter_context(nc.sbuf_tensor("v_sb", [128, S // 128, NHC * DH], BF16))

        with ExitStack() as esA:
            sbA = lambda n_, sh, dt: esA.enter_context(nc.sbuf_tensor(n_, sh, dt))
            wk_sb = sbA("wk_sb", [128, NKT, NHC * DH], BF16)
            wv_sb = sbA("wv_sb", [128, NKT, NHC * DH], BF16)
            xTc = [sbA(f"xTc{i}", [128, NKT, 512], BF16) for i in range(2)]
            cos_c = [sbA(f"cos_c{i}", [128, 4, 64], F32) for i in range(2)]
            sin_c = [sbA(f"sin_c{i}", [128, 4, 64], F32) for i in range(2)]
            t_scr = [sbA(f"t_scr{i}", [128, 4, 64], F32) for i in range(4)]
            ktmp = sbA("ktmp", [128, 512], BF16)
            psK = esA.enter_context(nc.psum_tensor("psK", [128, 512], F32))
            psV = esA.enter_context(nc.psum_tensor("psV", [128, 512], F32))
            psT = esA.enter_context(nc.psum_tensor("psT", [128, 512], BF16))

            p.op("gpsimd", lambda g: g.dma_start(
                out=wk_sb[:], in_=wkT.rearrange("(k p) m -> p k m", p=128)), gld, 16)
            p.op("gpsimd", lambda g: g.dma_start(
                out=wv_sb[:], in_=wvT.rearrange("(k p) m -> p k m", p=128)), gld, 16)
            wA_gld = gld.n


            xTc_user = [0, 0]
            cos_user = [0, 0]
            xTc_ld = [0, 0]
            cos_ld = [0, 0]
            psK_rd = 0
            psT_rd = 0
            psV_rd = 0
            ktmp_rd = 0

            for cch in range(16):
                bf = cch % 2
                p.wait("gpsimd", pe, xTc_user[bf])
                xTc_ld[bf] = p.op("gpsimd", lambda g, c=cch, b=bf: g.dma_start(
                    out=xTc[b][:], in_=xT[:, c * 512:(c + 1) * 512]
                    .rearrange("(k p) s -> p k s", p=128)), gld, 16)

                p.wait("sync", dve, cos_user[bf])
                p.op("sync", lambda sy, c=cch, b=bf: sy.dma_start(
                    out=cos_c[b][:], in_=cosT[c * 512:(c + 1) * 512, :]
                    .rearrange("(t p) f -> p t f", p=128)), sld, 16)
                cos_ld[bf] = p.op("sync", lambda sy, c=cch, b=bf: sy.dma_start(
                    out=sin_c[b][:], in_=sinT[c * 512:(c + 1) * 512, :]
                    .rearrange("(t p) f -> p t f", p=128)), sld, 16)

                for stl in range(4):
                    g_tile = cch * 4 + stl
                    p.wait("tensor", gld, max(xTc_ld[bf], wA_gld))
                    p.wait("tensor", dve, psK_rd)
                    for kt in range(NKT):
                        p.op("tensor", lambda t, b=bf, kt=kt, s=stl: t.matmul(
                            psK[:], xTc[b][:, kt, s * 128:(s + 1) * 128],
                            wk_sb[:, kt, :], start=(kt == 0), stop=(kt == NKT - 1)),
                            pe if kt == NKT - 1 else None)
                    peK = pe.n
                    p.wait("tensor", act, psV_rd)
                    for kt in range(NKT):
                        p.op("tensor", lambda t, b=bf, kt=kt, s=stl: t.matmul(
                            psV[:], xTc[b][:, kt, s * 128:(s + 1) * 128],
                            wv_sb[:, kt, :], start=(kt == 0), stop=(kt == NKT - 1)),
                            pe if kt == NKT - 1 else None)
                    peV = pe.n
                    p.wait("vector", pe, peK)
                    p.wait("vector", sld, cos_ld[bf])
                    p.wait("vector", pe, ktmp_rd)
                    ps3 = psK[:].rearrange("p (h x) -> p h x", h=4)
                    x1, x2 = ps3[:, :, 0:64], ps3[:, :, 64:128]
                    cb = AP(cos_c[bf], stl * 64, [[256, 128], [0, 4], [1, 64]])
                    sbn = AP(sin_c[bf], stl * 64, [[256, 128], [0, 4], [1, 64]])
                    p.op("vector", lambda v, x1=x1, cb=cb: v.tensor_mul(t_scr[0][:], x1, cb))
                    p.op("vector", lambda v, x2=x2, sbn=sbn: v.tensor_mul(t_scr[1][:], x2, sbn))
                    p.op("vector", lambda v, x1=x1, sbn=sbn: v.tensor_mul(t_scr[2][:], x1, sbn))
                    p.op("vector", lambda v, x2=x2, cb=cb: v.tensor_mul(t_scr[3][:], x2, cb), dve)
                    psK_rd = dve.n
                    p.wait("vector", dve, psK_rd)
                    kt3 = ktmp[:].rearrange("p (h x) -> p h x", h=4)
                    p.op("vector", lambda v, kt3=kt3: v.tensor_sub(
                        kt3[:, :, 0:64], t_scr[0][:], t_scr[1][:]))
                    p.op("vector", lambda v, kt3=kt3: v.tensor_add(
                        kt3[:, :, 64:128], t_scr[2][:], t_scr[3][:]), dve)
                    kt_ready = dve.n
                    p.op("scalar", lambda s_, v=peV: s_.wait_ge(pe.h, v))
                    p.op("scalar", lambda s_, g=g_tile: s_.copy(v_sb[:, g, :], psV[:]), act)
                    psV_rd = act.n
                    p.wait("tensor", dve, max(kt_ready, psT_rd))
                    for h in range(NHC):
                        p.op("tensor", lambda t, h=h: t.transpose(
                            psT[:, h * 128:(h + 1) * 128],
                            ktmp[:, h * 128:(h + 1) * 128], ident_sb[:]),
                            pe if h == NHC - 1 else None)
                    ktmp_rd = pe.n
                    peT = pe.n
                    p.wait("vector", pe, peT)
                    dstK = kT_sb[:, :, g_tile * 128:(g_tile + 1) * 128]
                    srcT = psT[:].rearrange("p (h x) -> p h x", h=4)
                    p.op("vector", lambda v, d=dstK, s_=srcT: v.tensor_copy(d, s_), dve)
                    psT_rd = dve.n
                xTc_user[bf] = pe.n
                cos_user[bf] = dve.n
            phaseA_pe = pe.n

        with ExitStack() as esB:
            sbB = lambda n_, sh, dt: esB.enter_context(nc.sbuf_tensor(n_, sh, dt))
            wq_sb = sbB("wq_sb", [128, NKT, NHC * DH], BF16)
            xTI = sbB("xTI", [128, NKT, 1024], BF16)
            cos_I = sbB("cos_I", [128, 8, 64], F32)
            sin_I = sbB("sin_I", [128, 8, 64], F32)
            tB = [sbB(f"tB{i}", [128, 4, 64], F32) for i in range(2)]
            qtmp = sbB("qtmp", [128, 512], BF16)
            qT_sb = sbB("qT_sb", [128, NHC, 1024], BF16)
            pt = [sbB(f"pt{i}", [128, 1024], BF16) for i in range(2)]
            rb_sb = sbB("rb_sb", [128, 1024], F32)
            rdb_sb = sbB("rdb_sb", [1, 1024], BF16)
            o_sb = sbB("o_sb", [128, 1024], BF16)

            p.wait("gpsimd", pe, phaseA_pe)
            p.op("gpsimd", lambda g: g.dma_start(
                out=wq_sb[:], in_=wqT.rearrange("(k p) m -> p k m", p=128)), gld, 16)
            wq_ld = gld.n
            p.wait("sync", pe, phaseA_pe)

            xTI_user = 0
            cosI_user = 0
            o_store = [0]
            attn_half_st = 0
            for I in range(8):
                p.wait("gpsimd", pe, xTI_user)
                xTI_ld = p.op("gpsimd", lambda g, I=I: g.dma_start(
                    out=xTI[:], in_=xT[:, I * 1024:(I + 1) * 1024]
                    .rearrange("(k p) s -> p k s", p=128)), gld, 16)
                p.wait("sync", dve, cosI_user)
                p.op("sync", lambda sy, I=I: sy.dma_start(
                    out=cos_I[:], in_=cosT[I * 1024:(I + 1) * 1024, :]
                    .rearrange("(t p) f -> p t f", p=128)), sld, 16)
                cosI_ld = p.op("sync", lambda sy, I=I: sy.dma_start(
                    out=sin_I[:], in_=sinT[I * 1024:(I + 1) * 1024, :]
                    .rearrange("(t p) f -> p t f", p=128)), sld, 16)

                superA_act, superA_dve = act.n, dve.n

                with ExitStack() as esQ:
                    psQ = esQ.enter_context(nc.psum_tensor(f"psQ{I}", [128, 512], F32))
                    psTq = esQ.enter_context(nc.psum_tensor(f"psTq{I}", [128, 512], BF16))
                    psQ_rd = 0
                    psTq_rd = 0
                    qtmp_rd = 0
                    first_q = True
                    for stl in range(8):
                        p.wait("tensor", gld, max(xTI_ld, wq_ld))
                        if first_q:
                            p.wait("tensor", act, superA_act)
                            p.wait("tensor", dve, superA_dve)
                            first_q = False
                        p.wait("tensor", dve, psQ_rd)
                        for kt in range(NKT):
                            p.op("tensor", lambda t, kt=kt, s=stl: t.matmul(
                                psQ[:], xTI[:, kt, s * 128:(s + 1) * 128],
                                wq_sb[:, kt, :], start=(kt == 0), stop=(kt == NKT - 1)),
                                pe if kt == NKT - 1 else None)
                        peQ = pe.n
                        p.wait("vector", pe, max(peQ, qtmp_rd))
                        p.wait("vector", sld, cosI_ld)
                        ps3 = psQ[:].rearrange("p (h x) -> p h x", h=4)
                        x1, x2 = ps3[:, :, 0:64], ps3[:, :, 64:128]
                        cb = AP(cos_I, stl * 64, [[512, 128], [0, 4], [1, 64]])
                        sbn = AP(sin_I, stl * 64, [[512, 128], [0, 4], [1, 64]])
                        p.op("vector", lambda v, x1=x1, cb=cb: v.tensor_mul(tB[0][:], x1, cb))
                        p.op("vector", lambda v, x2=x2, sbn=sbn: v.tensor_mul(tB[1][:], x2, sbn), dve)
                        p.wait("vector", dve, dve.n)
                        qt3 = qtmp[:].rearrange("p (h x) -> p h x", h=4)
                        p.op("vector", lambda v, qt3=qt3: v.tensor_sub(
                            qt3[:, :, 0:64], tB[0][:], tB[1][:]))
                        p.op("vector", lambda v, x1=x1, sbn=sbn: v.tensor_mul(tB[0][:], x1, sbn))
                        p.op("vector", lambda v, x2=x2, cb=cb: v.tensor_mul(tB[1][:], x2, cb), dve)
                        psQ_rd = dve.n
                        p.wait("vector", dve, psQ_rd)
                        p.op("vector", lambda v, qt3=qt3: v.tensor_add(
                            qt3[:, :, 64:128], tB[0][:], tB[1][:]), dve)
                        qt_ready = dve.n
                        p.wait("tensor", dve, max(qt_ready, psTq_rd))
                        for h in range(NHC):
                            p.op("tensor", lambda t, h=h: t.transpose(
                                psTq[:, h * 128:(h + 1) * 128],
                                qtmp[:, h * 128:(h + 1) * 128], ident_sb[:]),
                                pe if h == NHC - 1 else None)
                        qtmp_rd = pe.n
                        peTq = pe.n
                        p.wait("vector", pe, peTq)
                        dstQ = qT_sb[:, :, stl * 128:(stl + 1) * 128]
                        srcTq = psTq[:].rearrange("p (h x) -> p h x", h=4)
                        p.op("vector", lambda v, d=dstQ, s_=srcTq: v.tensor_copy(d, s_), dve)
                        psTq_rd = dve.n
                    qT_ready = dve.n
                if dbg and I == 7:
                    p.wait("sync", dve, qT_ready)
                    qT_st = p.op("sync", lambda sy: sy.dma_start(
                        out=dbg_qT.rearrange("(h p) q -> p h q", p=128), in_=qT_sb[:]), st, 16)
                xTI_user = pe.n
                cosI_user = dve.n

                with ExitStack() as esS:
                    ps_sc = [esS.enter_context(nc.psum_tensor(f"ps_sc{I}_{i}", [128, 1024], F32))
                             for i in range(2)]
                    ps_o = esS.enter_context(nc.psum_tensor(f"ps_o{I}", [128, 1024], F32))
                    ps_d = esS.enter_context(nc.psum_tensor(f"ps_d{I}", [1, 1024], F32))

                    sc_rd = [act.n, act.n]
                    pt_rd = [pe.n, pe.n]
                    o_rd = dve.n
                    d_rd = dve.n
                    bc_rd = dve.n
                    njj = 8 * I + 8

                    def emit_av(info):
                        (j, bf, qs, parts, exp_done, mask_done, first, last) = info
                        qoff = qs * 128
                        p.op("tensor", lambda t, v=exp_done: t.wait_ge(act.h, v))
                        if mask_done is not None:
                            p.op("tensor", lambda t, v=mask_done: t.wait_ge(dve.h, v))
                        for (o_, n_) in parts:
                            p.op("tensor", lambda t, b=bf, o_=o_, n_=n_, q=qoff, j=j, f=first, l=last, h=hh: t.matmul(
                                ps_o[:, q + o_:q + o_ + n_],
                                v_sb[:, j, h * 128:(h + 1) * 128],
                                pt[b][:, o_:o_ + n_], start=f, stop=l))
                        for ip, (o_, n_) in enumerate(parts):
                            p.op("tensor", lambda t, b=bf, o_=o_, n_=n_, q=qoff, f=first, l=last: t.matmul(
                                ps_d[0:1, q + o_:q + o_ + n_], ones_col[:],
                                pt[b][:, o_:o_ + n_], start=f, stop=l),
                                pe if ip == len(parts) - 1 else None)
                        pt_rd[bf] = pe.n

                    for hh in range(NHC):
                        pending = None
                        for j in range(njj):
                            bf = j % 2
                            qs = max(j - 8 * I, 0)
                            N = (8 - qs) * 128
                            qoff = qs * 128
                            parts = [(0, min(N, 512))]
                            if N > 512:
                                parts.append((512, N - 512))
                            p.wait("tensor", dve, qT_ready)
                            p.wait("tensor", act, sc_rd[bf])
                            if j <= 1:
                                p.wait("tensor", dve, max(bc_rd, d_rd, o_rd))
                            for ip, (o_, n_) in enumerate(parts):
                                p.op("tensor", lambda t, h=hh, j=j, b=bf, o_=o_, n_=n_, q=qoff: t.matmul(
                                    ps_sc[b][:, o_:o_ + n_],
                                    kT_sb[:, h, j * 128:(j + 1) * 128],
                                    qT_sb[:, h, q + o_:q + o_ + n_],
                                    start=True, stop=True),
                                    pe if ip == len(parts) - 1 else None)
                            sc_done = pe.n
                            if pending is not None:
                                emit_av(pending)
                            p.op("scalar", lambda s_, v=sc_done: s_.wait_ge(pe.h, v))
                            p.op("scalar", lambda s_, v=pt_rd[bf]: s_.wait_ge(pe.h, v))
                            p.op("scalar", lambda s_, b=bf, N=N: s_.activation(
                                pt[b][:, 0:N], ps_sc[b][:, 0:N], AF.Exp, scale=RSQ), act)
                            sc_rd[bf] = act.n
                            exp_done = act.n
                            mask_done = None
                            if dbg and I == 7 and hh == 0 and j == 0:
                                p.wait("sync", act, exp_done)
                                pt_st = p.op("sync", lambda sy, b=bf: sy.dma_start(
                                    out=dbg_pt, in_=pt[b][:]), st, 16)
                                p.wait("scalar", st, pt_st)
                            if j >= 8 * I:
                                p.wait("vector", act, exp_done)
                                p.op("vector", lambda v, b=bf: v.tensor_mul(
                                    pt[b][:, 0:128], pt[b][:, 0:128], diag_sb[:]), dve)
                                mask_done = dve.n
                            pending = (j, bf, qs, parts, exp_done, mask_done,
                                       j == 0, j == njj - 1)
                        emit_av(pending)
                        av_all = pe.n
                        p.wait("vector", pe, av_all)
                        def _recip(v):
                            with nc.allow_low_precision(reason="softmax 1/denom broadcast in bf16 (tolerance budgeted)"):
                                return v.reciprocal(rdb_sb[0:1, :], ps_d[0:1, :])
                        p.op("vector", _recip, dve)
                        d_rd = dve.n
                        if dbg:
                            p.wait("sync", dve, d_rd)
                            den_st = p.op("sync", lambda sy, I=I, h=hh: sy.dma_start(
                                out=dbg_den[I * 4 + h:I * 4 + h + 1, :],
                                in_=rb_sb[0:1, :]), st, 16)
                            p.wait("vector", st, den_st)
                        p.wait("tensor", dve, d_rd)
                        p.wait("tensor", act, sc_rd[1])
                        for ip, o_ in enumerate((0, 512)):
                            p.op("tensor", lambda t, o_=o_: t.matmul(
                                ps_sc[1][:, o_:o_ + 512], ones_rowb[:],
                                rdb_sb[0:1, o_:o_ + 512], start=True, stop=True),
                                pe if ip == 1 else None)
                        bc_done = pe.n
                        p.wait("vector", pe, bc_done)
                        p.op("vector", lambda v: v.tensor_copy(rb_sb[:], ps_sc[1][:]), dve)
                        p.wait("vector", dve, dve.n)
                        p.wait("vector", st, o_store[0])
                        p.op("vector", lambda v: v.tensor_mul(o_sb[:], ps_o[:], rb_sb[:]), dve)
                        o_rd = dve.n
                        bc_rd = o_rd
                        sc_rd[1] = act.n
                        p.wait("scalar", dve, o_rd)
                        p.op("scalar", lambda s_, h=hh, I=I: s_.dma_start(
                            out=attn_self.ap()[(I // 4) * 512 + h * 128:(I // 4) * 512 + (h + 1) * 128,
                                               (I % 4) * 1024:(I % 4 + 1) * 1024],
                            in_=o_sb[:]), st, 16)
                        o_store[0] = st.n
                if I == 3:
                    attn_half_st = st.n
                    p.wait("gpsimd", st, attn_half_st)
                    p.op("gpsimd", lambda g: g.collective_compute(
                        "AllGather", ALU.bypass, replica_groups=groups,
                        ins=[attn_self.ap()[0:512, :].opt()],
                        outs=[attn_gat.ap()[0:2048, :].opt()]), cc, 1)
            attn_done_st = st.n
        esAB.close()

        p.wait("gpsimd", st, attn_done_st)
        p.op("gpsimd", lambda g: g.collective_compute(
            "AllGather", ALU.bypass, replica_groups=groups,
            ins=[attn_self.ap()[512:1024, :].opt()],
            outs=[attn_gat.ap()[2048:4096, :].opt()]), cc, 1)
        cc_done = cc.n

        with ExitStack() as esC:
            sbC = lambda n_, sh, dt: esC.enter_context(nc.sbuf_tensor(n_, sh, dt))
            wo_sb = sbC("wo_sb", [128, NKT, D], BF16)
            htok_q = [sbC(f"htok_q{i}", [128, 4, D], BF16) for i in range(4)]

            def htok_ap(tt):
                return htok_q[tt // 4][:, tt % 4, :]
            ag_sb = sbC("ag_sb", [128, NKT, 512], BF16)
            xo_sb = [sbC(f"xo_sb{i}", [128, D], F32) for i in range(2)]
            hp_sb = sbC("hp_sb", [128, D], F32)
            sq_scr = sbC("sq_scr", [128, D], BF16)
            hTt_sb = sbC("hTt_sb", [128, NKT, 128], BF16)
            sum1 = sbC("sum1", [128, 1], F32)
            ssq = sbC("ssq", [128, 1], F32)
            negmean = sbC("negmean", [128, 1], F32)
            m2t = sbC("m2t", [128, 1], F32)
            var = sbC("var", [128, 1], F32)
            srt = sbC("srt", [128, 1], F32)
            rstd = sbC("rstd", [128, 1], F32)
            lmax = sbC("lmax", [128, 1], F32)
            l2 = sbC("l2", [128, E], F32)
            lmax2 = sbC("lmax2", [128, 1], F32)
            msk1 = sbC("msk1", [128, E], F32)
            msk2 = sbC("msk2", [128, E], F32)
            w1g = sbC("w1g", [128, 1], F32)
            w2g = sbC("w2g", [128, 1], F32)
            selb_sb = sbC("selb_sb", [128, 16, E], BF16)
            pfx_sb = sbC("pfx_sb", [128, 16, E], F32)
            tot_sb = sbC("tot_sb", [1, 128], F32)
            toff_sb = sbC("toff_sb", [1, 128], F32)
            btoff_sb = sbC("btoff_sb", [128, 16, E], F32)
            slotf_sb = sbC("slotf_sb", [128, E], F32)
            inv_sb = sbC("inv_sb", [128, E], F32)
            ovr_sb = sbC("ovr_sb", [128, E], F32)
            m1_all = sbC("m1_all", [128, 16, E], BF16)
            slotr_sb = sbC("slotr_sb", [128, E], F32)
            msk2t = sbC("msk2t", [128, E], F32)
            sl_sb = sbC("sl_sb", [128, 1], F32)
            ps_wo = esC.enter_context(nc.psum_tensor("ps_wo", [128, D], F32))
            psTc = esC.enter_context(nc.psum_tensor("psTc", [128, 512], BF16))
            ps_r = esC.enter_context(nc.psum_tensor("ps_r", [128, E], F32))
            ps_p = esC.enter_context(nc.psum_tensor("ps_p", [128, E], F32))
            ps_t1 = esC.enter_context(nc.psum_tensor("ps_t1", [1, E], F32))

            for i4 in range(4):
                p.op("gpsimd", lambda g, i4=i4: g.dma_start(
                    out=wo_sb[:, i4 * 4:(i4 + 1) * 4, :],
                    in_=woT[i4 * 512:(i4 + 1) * 512, :].rearrange("(k p) n -> p k n", p=128)),
                    gld, 16)
            wo_ld = gld.n

            entryC_act, entryC_dve = act.n, dve.n
            bregs = {}
            p.op("gpsimd", lambda g: bregs.__setitem__("scat", g.to_reg(NSL - 1)))
            p.op("gpsimd", lambda g: bregs.__setitem__("ret", g.to_reg(2 * TOK - 1)))
            tgrow = sbC("tgrow", [1, NSL // 4], F32)
            ztile = sbC("ztile", [128, D], BF16)
            p.op("gpsimd", lambda g: g.memset(tgrow[:], 1e6))
            p.op("gpsimd", lambda g: g.memset(ztile[:], 0.0), pool)
            tgm_done = pool.n
            p.op("scalar", lambda s_, v=tgm_done: s_.wait_ge(pool.h, v))
            for zi in range(8):
                p.op("scalar", lambda s_, zi=zi: s_.dma_start(
                    out=tg_dram.ap().rearrange("(a r) c -> a (r c)", a=8)[zi:zi + 1, :],
                    in_=tgrow[0:1, :]), st, 16)
            for zi in range(2 * TOK // 128):
                p.op("scalar", lambda s_, zi=zi: s_.dma_start(
                    out=y2tok.ap()[zi * 128:(zi + 1) * 128, :], in_=ztile[:]), st, 16)
            prefill_st = st.n
            ag_ld = 0
            ag_user = 0
            hp_rd = 0
            wo_add = 0
            psTc_rd = 0
            htok_rd_pe = 0
            gate_done = 0
            xo_ld = [0, 0]
            xo_user = [0, 0]
            firstC = True
            ps_p_rd = 0
            ps_t1_rd = 0
            h_st = 0
            scat_done = [0, 0]
            slotC_done = 0

            for c4 in range(4):
                p.wait("gpsimd", cc, cc_done)
                p.wait("gpsimd", pe, ag_user)
                for kt in range(NKT):
                    p.op("gpsimd", lambda g, kt=kt, c4=c4: g.indirect_dma_start(
                        ag_sb[:, kt, :], None,
                        attn_gat.ap()[0:128, 0:512],
                        IndirectOffsetOnAxis(agidx_sb[:, kt:kt + 1], 1),
                        element_offset=c4 * 512), gld, 16)
                ag_ld = gld.n
                for tsub in range(4):
                    tt = c4 * 4 + tsub
                    xb = tt % 2
                    p.wait("sync", cc, cc_done)
                    p.wait("sync", dve, xo_user[xb])
                    xo_ld[xb] = p.op("sync", lambda sy, tt=tt, b=xb: sy.dma_start(
                        out=xo_sb[b][:], in_=x_own[tt * 128:(tt + 1) * 128, :]), sld, 16)
                    p.wait("tensor", gld, max(ag_ld, wo_ld))
                    if firstC:
                        p.wait("tensor", act, entryC_act)
                        p.wait("tensor", dve, entryC_dve)
                        firstC = False
                    p.wait("tensor", dve, wo_add)
                    for n4 in range(4):
                        for kt in range(NKT):
                            p.op("tensor", lambda t, n4=n4, kt=kt, ts=tsub: t.matmul(
                                ps_wo[:, n4 * 512:(n4 + 1) * 512],
                                ag_sb[:, kt, ts * 128:(ts + 1) * 128],
                                wo_sb[:, kt, n4 * 512:(n4 + 1) * 512],
                                start=(kt == 0), stop=(kt == NKT - 1)),
                                pe if (n4 == 3 and kt == NKT - 1) else None)
                    wo_done = pe.n
                    p.wait("vector", pe, wo_done)
                    p.wait("vector", sld, xo_ld[xb])
                    p.wait("vector", act, hp_rd)
                    p.op("vector", lambda v, b=xb: v.tensor_add(hp_sb[:], xo_sb[b][:], ps_wo[:]), dve)
                    a1 = dve.n
                    wo_add = a1
                    xo_user[xb] = a1
                    p.op("scalar", lambda s_, v=a1: s_.wait_ge(dve.h, v))
                    p.op("scalar", lambda s_: s_.activation(
                        sq_scr[:], hp_sb[:], AF.Square, accum_out=ssq[:]), act)
                    hp_rd = act.n
                    sq_done = act.n
                    p.wait("vector", dve, a1)
                    p.op("vector", lambda v: v.reduce_sum(sum1[:], hp_sb[:], axis=AX.X), dve)
                    a2 = dve.n
                    p.wait("vector", dve, a2)
                    p.op("vector", lambda v: v.tensor_scalar(
                        negmean[:], sum1[:], -1.0 / D, None, ALU.mult), dve)
                    a3 = dve.n
                    p.wait("vector", dve, a3)
                    p.op("vector", lambda v: v.tensor_mul(m2t[:], negmean[:], negmean[:]))
                    p.wait("vector", act, sq_done)
                    p.op("vector", lambda v: v.tensor_scalar(
                        var[:], ssq[:], 1.0 / D, None, ALU.mult), dve)
                    a4 = dve.n
                    p.wait("vector", dve, a4)
                    p.op("vector", lambda v: v.tensor_sub(var[:], var[:], m2t[:]), dve)
                    a5 = dve.n
                    p.op("scalar", lambda s_, v=a5: s_.wait_ge(dve.h, v))
                    p.op("scalar", lambda s_: s_.activation(
                        srt[:], var[:], AF.Sqrt, bias=eps_sb[:]), act)
                    a6 = act.n
                    p.wait("vector", act, a6)
                    p.op("vector", lambda v: v.reciprocal(rstd[:], srt[:]), dve)
                    a7 = dve.n
                    p.wait("vector", dve, a7)
                    p.op("vector", lambda v, tt=tt: v.tensor_scalar(
                        htok_ap(tt), hp_sb[:], negmean[:], rstd[:],
                        ALU.add, ALU.mult), dve)
                    htok_done = dve.n
                    p.wait("scalar", dve, htok_done)
                    h_st = p.op("scalar", lambda s_, tt=tt: s_.dma_start(
                        out=h_tok_dram.ap()[tt * 128:(tt + 1) * 128, :],
                        in_=htok_ap(tt)), st, 16)
                    for ktg in range(4):
                        p.wait("tensor", dve, max(htok_done, psTc_rd))
                        p.wait("tensor", pe, htok_rd_pe)
                        for i in range(4):
                            p.op("tensor", lambda t, ktg=ktg, i=i, tt=tt: t.transpose(
                                psTc[:, i * 128:(i + 1) * 128],
                                htok_ap(tt)[:, (ktg * 4 + i) * 128:(ktg * 4 + i + 1) * 128],
                                ident_sb[:]), pe if i == 3 else None)
                        trd = pe.n
                        p.wait("vector", pe, trd)
                        dstH = hTt_sb[:, ktg * 4:(ktg + 1) * 4, :]
                        srcH = psTc[:].rearrange("p (k x) -> p k x", k=4)
                        p.op("vector", lambda v, d=dstH, s_=srcH: v.tensor_copy(d, s_), dve)
                        psTc_rd = dve.n
                    hT_done = dve.n
                    p.wait("tensor", dve, max(hT_done, gate_done))
                    for kt in range(NKT):
                        p.op("tensor", lambda t, kt=kt: t.matmul(
                            ps_r[:], hTt_sb[:, kt, :],
                            routerT_sb[:, kt, :], start=(kt == 0), stop=(kt == NKT - 1)),
                            pe if kt == NKT - 1 else None)
                    r_done = pe.n
                    htok_rd_pe = 0
                    p.wait("vector", pe, r_done)
                    p.op("vector", lambda v: v.reduce_max(lmax[:], ps_r[:], axis=AX.X), dve)
                    b1 = dve.n
                    p.wait("vector", dve, b1)
                    p.op("vector", lambda v: v.tensor_scalar(
                        msk1[:], ps_r[:], lmax[:], None, ALU.is_ge), dve)
                    b2 = dve.n
                    p.wait("vector", dve, b2)
                    p.op("vector", lambda v, tt=tt: v.tensor_copy(m1_all[:, tt, :], msk1[:]))
                    p.op("vector", lambda v: v.scalar_tensor_tensor(
                        l2[:], msk1[:], -1e30, ps_r[:], ALU.mult, ALU.add), dve)
                    b3 = dve.n
                    p.wait("vector", dve, b3)
                    p.op("vector", lambda v: v.reduce_max(lmax2[:], l2[:], axis=AX.X), dve)
                    b4 = dve.n
                    p.wait("vector", dve, b4)
                    p.op("vector", lambda v: v.tensor_scalar(
                        msk2[:], l2[:], lmax2[:], None, ALU.is_ge), dve)
                    b5 = dve.n
                    p.op("scalar", lambda s_, v=b5: s_.wait_ge(dve.h, v))
                    p.op("scalar", lambda s_: s_.activation(
                        w1g[:], lmax2[:], AF.Sigmoid, bias=lmax[:], scale=-1.0), act)
                    b6 = act.n
                    p.wait("vector", act, b6)
                    p.op("vector", lambda v, tt=tt: v.tensor_copy(
                        vg_sb[:, tt, 0, 1:2], w1g[:]))
                    p.op("vector", lambda v: v.tensor_scalar(
                        w2g[:], w1g[:], -1.0, 1.0, ALU.mult, ALU.add), dve)
                    b7 = dve.n
                    p.wait("vector", dve, b7)
                    p.op("vector", lambda v, tt=tt: v.tensor_copy(
                        vg_sb[:, tt, 1, 1:2], w2g[:]))
                    p.op("vector", lambda v, tt=tt: v.tensor_add(
                        selb_sb[:, tt, :], msk1[:], msk2[:]), dve)
                    gate_done = dve.n
                    p.wait("tensor", dve, max(gate_done, ps_p_rd, ps_t1_rd))
                    p.op("tensor", lambda t, tt=tt: t.matmul(
                        ps_p[:], ltc_sb[:], selb_sb[:, tt, :], start=True, stop=True))
                    p.op("tensor", lambda t, tt=tt: t.matmul(
                        ps_t1[:], ones_col[:], selb_sb[:, tt, :], start=True, stop=True),
                        pe)
                    pfx_done = pe.n
                    p.wait("vector", pe, pfx_done)
                    p.op("vector", lambda v, tt=tt: v.tensor_copy(pfx_sb[:, tt, :], ps_p[:]))
                    p.op("vector", lambda v, tt=tt: v.tensor_copy(
                        tot_sb[0:1, tt * 8:(tt + 1) * 8], ps_t1[0:1, :]), dve)
                    ps_p_rd = dve.n
                    ps_t1_rd = dve.n

                    if tt in (7, 15):
                        hh2 = tt // 8
                        base_col = hh2 * 64
                        p.wait("vector", dve, dve.n)
                        p.op("vector", lambda v, bc=base_col: v.tensor_scalar(
                            toff_sb[0:1, bc:bc + 8], tot_sb[0:1, bc:bc + 8],
                            0.0, None, ALU.mult), dve)
                        for k in range(1, 8):
                            p.wait("vector", dve, dve.n)
                            p.op("vector", lambda v, bc=base_col, k=k: v.tensor_add(
                                toff_sb[0:1, bc + k * 8:bc + (k + 1) * 8],
                                toff_sb[0:1, bc + (k - 1) * 8:bc + k * 8],
                                tot_sb[0:1, bc + (k - 1) * 8:bc + k * 8]), dve)
                        p.wait("vector", dve, dve.n)
                        p.op("vector", lambda v, bc=base_col: v.tensor_add(
                            toff_sb[0:1, bc:bc + 64],
                            toff_sb[0:1, bc:bc + 64],
                            sbase_sb[0:1, bc:bc + 64]), dve)
                        toff_done = dve.n
                        p.wait("tensor", dve, max(toff_done, wo_add))
                        p.op("tensor", lambda t, bc=base_col: t.matmul(
                            ps_wo[:, 0:64], ones_row[:],
                            toff_sb[0:1, bc:bc + 64], start=True, stop=True), pe)
                        bc_done2 = pe.n
                        p.wait("vector", pe, bc_done2)
                        p.op("vector", lambda v, h2=hh2: v.tensor_copy(
                            btoff_sb[:, h2 * 8:(h2 + 1) * 8, :],
                            ps_wo[:, 0:64].rearrange("p (a b) -> p a b", b=8)), dve)
                        btoff_done = dve.n
                        wo_add = dve.n
                        for t2 in range(hh2 * 8, hh2 * 8 + 8):
                            p.wait("vector", dve, dve.n)
                            p.op("vector", lambda v, t2=t2: v.tensor_add(
                                slotf_sb[:], pfx_sb[:, t2, :], btoff_sb[:, t2, :]), dve)
                            p.wait("vector", dve, dve.n)
                            p.op("vector", lambda v: v.tensor_scalar(
                                slotr_sb[:], slotf_sb[:], -1.0, None, ALU.add), dve)
                            p.wait("vector", dve, dve.n)
                            p.op("vector", lambda v, t2=t2: v.tensor_mul(
                                inv_sb[:], m1_all[:, t2, :], slotr_sb[:]), dve)
                            p.wait("vector", dve, dve.n)
                            p.op("vector", lambda v: v.reduce_sum(sl_sb[:], inv_sb[:], axis=AX.X), dve)
                            p.wait("vector", dve, dve.n)
                            p.op("vector", lambda v, t2=t2: v.tensor_copy(
                                slot12_sb[:, t2, 0:1], sl_sb[:]))
                            p.op("vector", lambda v, t2=t2: v.tensor_sub(
                                msk2t[:], selb_sb[:, t2, :], m1_all[:, t2, :]), dve)
                            p.wait("vector", dve, dve.n)
                            p.op("vector", lambda v: v.tensor_mul(inv_sb[:], msk2t[:], slotr_sb[:]), dve)
                            p.wait("vector", dve, dve.n)
                            p.op("vector", lambda v: v.reduce_sum(sl_sb[:], inv_sb[:], axis=AX.X), dve)
                            p.wait("vector", dve, dve.n)
                            p.op("vector", lambda v, t2=t2: v.tensor_copy(
                                slot12_sb[:, t2, 1:2], sl_sb[:]))
                            p.op("vector", lambda v, t2=t2: v.tensor_scalar(
                                vg_sb[:, t2, 0, 0:1], iota_sb[:], 2.0,
                                float(t2 * 256 + 0), ALU.mult, ALU.add))
                            p.op("vector", lambda v, t2=t2: v.tensor_scalar(
                                vg_sb[:, t2, 1, 0:1], iota_sb[:], 2.0,
                                float(t2 * 256 + 1), ALU.mult, ALU.add), dve)
                            p.wait("gpsimd", dve, dve.n)
                            if t2 == 0:
                                p.wait("gpsimd", st, prefill_st)
                            for k2 in range(2):
                                p.op("gpsimd", lambda g, t2=t2, k2=k2: g.indirect_dma_start(
                                    xs_dram.ap()[0:128, :],
                                    IndirectOffsetOnAxis(slot12_sb[:, t2, k2:k2 + 1], 0),
                                    htok_ap(t2), None,
                                    bounds_check=bregs["scat"], oob_is_err=False), gld, 16)
                                p.op("gpsimd", lambda g, t2=t2, k2=k2: g.indirect_dma_start(
                                    tg_dram.ap()[0:128, :],
                                    IndirectOffsetOnAxis(slot12_sb[:, t2, k2:k2 + 1], 0),
                                    vg_sb[:, t2, k2, :], None,
                                    bounds_check=bregs["scat"], oob_is_err=False), gld, 16)
                        p.wait("gpsimd", gld, gld.n)
                        p.op("gpsimd", lambda g: g.nop(), scats)
                        scat_done[hh2] = scats.n
                ag_user = pe.n
            phaseC_pe = pe.n
            phaseC_act = act.n
            phaseC_dve = dve.n

        with ExitStack() as esD:
            sbD = lambda n_, sh, dt: esD.enter_context(nc.sbuf_tensor(n_, sh, dt))
            w1c = [sbD(f"w1c{i}", [128, NKT, 256], BF16) for i in range(2)]
            w2e = sbD("w2e", [128, NCI, D], BF16)
            xst = [sbD(f"xst{i}", [128, D], BF16) for i in range(2)]
            xsT = sbD("xsT", [128, NKT, 6 * 128], BF16)
            yact = sbD("yact", [128, NCI, 6 * 128], BF16)
            sact = sbD("sact", [128, 6 * 128], BF16)
            ysb = sbD("ysb", [128, D], BF16)
            ysb2 = [sbD(f"ysb2_{i}", [128, D], BF16) for i in range(2)]
            tg_sb = sbD("tg_sb", [128, 6, 2], F32)
            idx_sb = sbD("idx_sb", [128, 6, 1], I32)
            ps_g = esD.enter_context(nc.psum_tensor("ps_g", [128, 6 * 128], F32))
            ps_u = esD.enter_context(nc.psum_tensor("ps_u", [128, 6 * 128], F32))
            ps_y = [esD.enter_context(nc.psum_tensor(f"ps_y{i}", [128, 512], F32))
                    for i in range(2)]
            psTd = [esD.enter_context(nc.psum_tensor(f"psTd{i}", [128, 512], BF16))
                    for i in range(2)]

            entryD_act, entryD_dve, entryD_pe = act.n, dve.n, pe.n
            p.wait("gpsimd", scats, scat_done[1])
            p.wait("gpsimd", pe, pe.n)
            p.wait("gpsimd", act, act.n)
            p.wait("gpsimd", dve, dve.n)
            p.wait("gpsimd", st, st.n)

            w1_ld = [0, 0]
            w1_user = [pe.n, pe.n]
            w2_user = pe.n
            w2_ld = 0
            xst_ld = [0, 0]
            xst_user = [pe.n, pe.n]
            xsT_user = pe.n
            psTd_rd = [dve.n, dve.n]
            ps_g_rd = act.n
            ps_u_rd = dve.n
            ps_y_rd = [act.n, act.n]
            ysb_mul = dve.n
            ysb2_scat = [0, 0]
            idx_user = [0] * 6
            tg_user = [dve.n] * 6
            tg_ld = 0
            firstD = True

            def load_w1(e, ci):
                wb = ci % 2
                p.wait("gpsimd", pe, w1_user[wb])
                p.op("gpsimd", lambda g, e=e, ci=ci, b=wb: g.dma_start(
                    out=w1c[b][:], in_=w1P[e, ci]), gld, 16)
                w1_ld[wb] = gld.n

            def load_w2(e):
                for ci in range(NCI):
                    if ci == 0:
                        p.wait("gpsimd", pe, w2_user)
                    p.op("gpsimd", lambda g, e=e, ci=ci: g.dma_start(
                        out=w2e[:, ci, :],
                        in_=w2T[e, ci * 128:(ci + 1) * 128, :]), gld, 16)
                return gld.n

            for e in range(E):
                if e == 0:
                    w2_ld = load_w2(0)
                    load_w1(e, 0)
                    load_w1(e, 1)
                else:
                    w2_ld = w2_ld_next
                for g6 in range(6):
                    hh2, rt = g6 // 3, g6 % 3
                    xb2 = g6 % 2
                    row0 = hh2 * (E * CAP) + e * CAP + rt * 128
                    p.wait("sync", scats, scat_done[hh2])
                    p.wait("sync", pe, xst_user[xb2])
                    xst_ld[xb2] = p.op("sync", lambda sy, r0=row0, b=xb2: sy.dma_start(
                        out=xst[b][:], in_=xs_dram.ap()[r0:r0 + 128, :]), sld, 16)
                    p.wait("tensor", sld, xst_ld[xb2])
                    if firstD:
                        p.wait("tensor", act, entryD_act)
                        p.wait("tensor", dve, entryD_dve)
                        firstD = False
                    p.wait("tensor", pe, xsT_user)
                    for ktg in range(4):
                        tb2 = ktg % 2
                        p.wait("tensor", dve, psTd_rd[tb2])
                        for i in range(4):
                            p.op("tensor", lambda t, ktg=ktg, i=i, b=xb2, tb2=tb2: t.transpose(
                                psTd[tb2][:, i * 128:(i + 1) * 128],
                                xst[b][:, (ktg * 4 + i) * 128:(ktg * 4 + i + 1) * 128],
                                ident_sb[:]), pe if i == 3 else None)
                        trd = pe.n
                        if ktg == 3:
                            xst_user[xb2] = trd
                        p.wait("vector", pe, trd)
                        dstX = xsT[:, ktg * 4:(ktg + 1) * 4, g6 * 128:(g6 + 1) * 128]
                        srcX = psTd[tb2][:].rearrange("p (k x) -> p k x", k=4)
                        p.op("vector", lambda v, d=dstX, s_=srcX: v.tensor_copy(d, s_), dve)
                        psTd_rd[tb2] = dve.n
                xsT_done = dve.n
                if dbg and e == 0:
                    p.wait("sync", dve, xsT_done)
                    xsT_st = p.op("sync", lambda sy: sy.dma_start(out=dbg_xsT, in_=xsT[:]), st, 16)
                    p.wait("vector", st, xsT_st)
                    p.wait("tensor", st, xsT_st)
                for g6 in range(6):
                    hh2 = g6 // 3
                    row0 = hh2 * (E * CAP) + e * CAP + (g6 % 3) * 128
                    p.wait("sync", dve, tg_user[g6])
                    p.op("sync", lambda sy, r0=row0, g6=g6: sy.dma_start(
                        out=tg_sb[:, g6, :], in_=tg_dram.ap()[r0:r0 + 128, :]), sld, 16)
                tg_ld = sld.n
                for ci in range(NCI):
                    wb = ci % 2
                    p.wait("tensor", gld, w1_ld[wb])
                    p.wait("tensor", dve, xsT_done)
                    p.wait("tensor", act, ps_g_rd)
                    for (o_, n_) in ((0, 512), (512, 256)):
                        for kt in range(NKT):
                            p.op("tensor", lambda t, b=wb, kt=kt, o_=o_, n_=n_: t.matmul(
                                ps_g[:, o_:o_ + n_],
                                w1c[b][:, kt, 0:128],
                                xsT[:, kt, o_:o_ + n_],
                                start=(kt == 0), stop=(kt == NKT - 1)),
                                pe if (o_ == 512 and kt == NKT - 1) else None)
                    g_done = pe.n
                    p.wait("tensor", dve, ps_u_rd)
                    for (o_, n_) in ((0, 512), (512, 256)):
                        for kt in range(NKT):
                            p.op("tensor", lambda t, b=wb, kt=kt, o_=o_, n_=n_: t.matmul(
                                ps_u[:, o_:o_ + n_],
                                w1c[b][:, kt, 128:256],
                                xsT[:, kt, o_:o_ + n_],
                                start=(kt == 0), stop=(kt == NKT - 1)),
                                pe if (o_ == 512 and kt == NKT - 1) else None)
                    u_done = pe.n
                    w1_user[wb] = pe.n
                    if ci + 2 < NCI:
                        load_w1(e, ci + 2)
                    elif e + 1 < E and ci + 2 < NCI + 2:
                        load_w1(e + 1, (ci + 2) % 2)
                    p.op("scalar", lambda s_, v=g_done: s_.wait_ge(pe.h, v))
                    p.op("scalar", lambda s_, v=ps_u_rd: s_.wait_ge(dve.h, v))
                    p.op("scalar", lambda s_: s_.activation(sact[:], ps_g[:], AF.Silu), act)
                    ps_g_rd = act.n
                    silu_done = act.n
                    p.wait("vector", pe, u_done)
                    p.wait("vector", act, silu_done)
                    p.op("vector", lambda v, ci=ci: v.tensor_mul(
                        yact[:, ci, :], sact[:], ps_u[:]), dve)
                    ps_u_rd = dve.n
                yact_done = dve.n
                if dbg and e == 0:
                    p.wait("sync", dve, yact_done)
                    ya2_st = p.op("sync", lambda sy: sy.dma_start(out=dbg_ya2, in_=yact[:]), st, 16)
                    p.wait("tensor", st, ya2_st)
                    p.wait("vector", st, ya2_st)
                xsT_user = pe.n
                for g6 in range(6):
                    b2 = g6 % 2
                    for n4 in range(4):
                        yb = n4 % 2
                        p.wait("tensor", gld, w2_ld)
                        p.wait("tensor", dve, yact_done)
                        p.wait("tensor", act, ps_y_rd[yb])
                        for ci in range(NCI):
                            p.op("tensor", lambda t, ci=ci, g6=g6, n4=n4, b=yb: t.matmul(
                                ps_y[b][:],
                                yact[:, ci, g6 * 128:(g6 + 1) * 128],
                                w2e[:, ci, n4 * 512:(n4 + 1) * 512],
                                start=(ci == 0), stop=(ci == NCI - 1)),
                                pe if ci == NCI - 1 else None)
                        y2_done = pe.n
                        p.op("scalar", lambda s_, v=y2_done: s_.wait_ge(pe.h, v))
                        if n4 <= 1:
                            p.op("scalar", lambda s_, v=ysb_mul: s_.wait_ge(dve.h, v))
                        p.op("scalar", lambda s_, n4=n4, b=yb: s_.copy(
                            ysb[:, n4 * 512:(n4 + 1) * 512], ps_y[b][:]), act)
                        ps_y_rd[yb] = act.n
                    ys_evac = act.n
                    p.wait("vector", act, ys_evac)
                    p.wait("vector", sld, tg_ld)
                    p.wait("vector", gld, max(ysb2_scat[b2], idx_user[g6]))
                    p.op("vector", lambda v, g6=g6: v.tensor_copy(
                        idx_sb[:, g6, :], tg_sb[:, g6, 0:1]))
                    p.op("vector", lambda v, g6=g6, b2=b2: v.tensor_scalar(
                        ysb2[b2][:], ysb[:], tg_sb[:, g6, 1:2], None, ALU.mult), dve)
                    ysb_mul = dve.n
                    tg_user[g6] = dve.n
                    p.wait("gpsimd", dve, ysb_mul)
                    p.op("gpsimd", lambda g, g6=g6, b2=b2: g.indirect_dma_start(
                        y2tok.ap()[0:128, :],
                        IndirectOffsetOnAxis(idx_sb[:, g6, 0:1], 0),
                        ysb2[b2][:], None,
                        bounds_check=bregs["ret"], oob_is_err=False), gld, 16)
                    ysb2_scat[b2] = gld.n
                    idx_user[g6] = gld.n
                w2_user = pe.n
                if e + 1 < E:
                    w2_ld_next = load_w2(e + 1)
            y_done = gld.n
            lastD_act = act.n

        with ExitStack() as esE:
            sbE = lambda n_, sh, dt: esE.enter_context(nc.sbuf_tensor(n_, sh, dt))
            hE = [sbE(f"hE{i}", [128, D], BF16) for i in range(2)]
            yE = [sbE(f"yE{i}", [128, 2, D], BF16) for i in range(2)]
            hp2 = sbE("hp2", [128, D], F32)
            outt = [sbE(f"outt{i}", [128, D], F32) for i in range(2)]
            sqE = sbE("sqE", [128, D], BF16)
            sum1e = sbE("sum1e", [128, 1], F32)
            ssqe = sbE("ssqe", [128, 1], F32)
            negme = sbE("negme", [128, 1], F32)
            m2te = sbE("m2te", [128, 1], F32)
            vare = sbE("vare", [128, 1], F32)
            srte = sbE("srte", [128, 1], F32)
            rstde = sbE("rstde", [128, 1], F32)

            hE_ld = [0, 0]
            hE_user = [0, 0]
            hp2_sq = act.n
            out_user = [st.n, st.n]
            y2r = y2tok.ap().rearrange("(t k) d -> k t d", k=2)
            p.wait("sync", pe, pe.n)
            p.wait("sync", dve, dve.n)
            p.wait("sync", act, lastD_act)
            p.wait("sync", gld, y_done)
            for tt in range(16):
                ob = tt % 2
                p.wait("sync", st, h_st)
                p.wait("sync", dve, hE_user[ob])
                p.op("sync", lambda sy, tt=tt, b=ob: sy.dma_start(
                    out=hE[b][:], in_=h_tok_dram.ap()[tt * 128:(tt + 1) * 128, :]), sld, 16)
                for kk in range(2):
                    p.op("sync", lambda sy, tt=tt, b=ob, kk=kk: sy.dma_start(
                        out=yE[b][:, kk, :],
                        in_=y2r[kk, tt * 128:(tt + 1) * 128, :]), sld, 16)
                hE_ld[ob] = sld.n
                p.wait("vector", sld, hE_ld[ob])
                p.wait("vector", act, hp2_sq)
                p.op("vector", lambda v, b=ob: v.tensor_add(
                    hp2[:], hE[b][:], yE[b][:, 0, :]), dve)
                p.wait("vector", dve, dve.n)
                p.op("vector", lambda v, b=ob: v.tensor_add(
                    hp2[:], hp2[:], yE[b][:, 1, :]), dve)
                c1_ = dve.n
                hE_user[ob] = c1_
                p.op("scalar", lambda s_, v=c1_: s_.wait_ge(dve.h, v))
                p.op("scalar", lambda s_: s_.activation(
                    sqE[:], hp2[:], AF.Square, accum_out=ssqe[:]), act)
                hp2_sq = act.n
                sqe_done = act.n
                p.wait("vector", dve, c1_)
                p.op("vector", lambda v: v.reduce_sum(sum1e[:], hp2[:], axis=AX.X), dve)
                c2_ = dve.n
                p.wait("vector", dve, c2_)
                p.op("vector", lambda v: v.tensor_scalar(
                    negme[:], sum1e[:], -1.0 / D, None, ALU.mult), dve)
                c3_ = dve.n
                p.wait("vector", dve, c3_)
                p.op("vector", lambda v: v.tensor_mul(m2te[:], negme[:], negme[:]))
                p.wait("vector", act, sqe_done)
                p.op("vector", lambda v: v.tensor_scalar(
                    vare[:], ssqe[:], 1.0 / D, None, ALU.mult), dve)
                c4_ = dve.n
                p.wait("vector", dve, c4_)
                p.op("vector", lambda v: v.tensor_sub(vare[:], vare[:], m2te[:]), dve)
                c5_ = dve.n
                p.op("scalar", lambda s_, v=c5_: s_.wait_ge(dve.h, v))
                p.op("scalar", lambda s_: s_.activation(
                    srte[:], vare[:], AF.Sqrt, bias=eps_sb[:]), act)
                c6_ = act.n
                p.wait("vector", act, c6_)
                p.op("vector", lambda v: v.reciprocal(rstde[:], srte[:]), dve)
                c7_ = dve.n
                p.wait("vector", dve, c7_)
                p.wait("vector", st, out_user[ob])
                p.op("vector", lambda v, b=ob: v.tensor_scalar(
                    outt[b][:], hp2[:], negme[:], rstde[:], ALU.add, ALU.mult), dve)
                c8_ = dve.n
                p.wait("scalar", dve, c8_)
                p.op("scalar", lambda s_, tt=tt, b=ob: s_.dma_start(
                    out=out[tt * 128:(tt + 1) * 128, :], in_=outt[b][:]), st, 16)
                out_user[ob] = st.n
        out_st_total = st.n

        if dbg:
            p.wait("sync", cc, cc_done)
            p.wait("sync", st, out_st_total)
            p.op("sync", lambda sy: sy.dma_start(out=dbg_attn, in_=attn_gat.ap()), st, 16)
            p.op("sync", lambda sy: sy.dma_start(
                out=dbg_h.rearrange("t d -> t d"), in_=h_tok_dram.ap()), st, 16)
            p.op("sync", lambda sy: sy.dma_start(out=dbg_s12, in_=slot12_sb[:]), st, 16)
            p.op("sync", lambda sy: sy.dma_start(out=dbg_xs, in_=xs_dram.ap()), st, 16)
            p.op("sync", lambda sy: sy.dma_start(out=dbg_tg, in_=tg_dram.ap()), st, 16)
            p.op("sync", lambda sy: sy.dma_start(out=dbg_y2t, in_=y2tok.ap()), st, 16)

        p.wait("sync", st, st.n)

        block = es.enter_context(nc.Block())

        @block.tensor
        def _(t):
            for fn in p.streams["tensor"]:
                fn(t)

        @block.vector
        def _(v):
            for fn in p.streams["vector"]:
                fn(v)

        @block.scalar
        def _(s_):
            for fn in p.streams["scalar"]:
                fn(s_)

        @block.gpsimd
        def _(g):
            for fn in p.streams["gpsimd"]:
                fn(g)

        @block.sync
        def _(sy):
            for fn in p.streams["sync"]:
                fn(sy)

    nc._plan_streams = {k: list(v) for k, v in p.streams.items()}
    return nc


_ROPE_PERM = np.concatenate([np.arange(0, 128, 2), np.arange(1, 128, 2)])


def _host_prepare(x, Wq, Wk, Wv, Wo, router_w, w1, w2):
    x = np.asarray(x, dtype=np.float32)
    xT = [np.ascontiguousarray(x[b].T) for b in range(2)]
    wq_s, wk_s, wv_s = [], [], []
    for hg in range(4):
        def permslice(W):
            blk = np.asarray(W)[hg * 512:(hg + 1) * 512].reshape(4, 128, D)[:, _ROPE_PERM, :]
            return np.ascontiguousarray(blk.reshape(512, D).T)
        wq_s.append(permslice(Wq))
        wk_s.append(permslice(Wk))
        wv_s.append(np.ascontiguousarray(
            np.asarray(Wv)[hg * 512:(hg + 1) * 512].T))
    woT = np.ascontiguousarray(np.asarray(Wo).T)
    routerT = np.ascontiguousarray(np.asarray(router_w).T)
    _A = np.asarray(w1).reshape(E, 22, 128, NKT, 128).transpose(0, 1, 4, 3, 2)
    w1P = np.ascontiguousarray(
        np.stack([_A[:, :NCI], _A[:, NCI:]], axis=4)).reshape(E, NCI, 128, NKT, 256)
    w2T = np.ascontiguousarray(np.asarray(w2).transpose(0, 2, 1))
    inv_freq = (1.0 / (10000.0 ** (np.arange(0, 128, 2, dtype=np.float32) / 128.0))).astype(np.float32)
    t = np.arange(S, dtype=np.float32)
    freqs = np.outer(t, inv_freq).astype(np.float32)
    cosT = np.cos(freqs).astype(np.float32)
    sinT = np.sin(freqs).astype(np.float32)
    ident = np.eye(128, dtype=np.float32)
    kk, qq = np.meshgrid(np.arange(128), np.arange(128), indexing="ij")
    diag = (kk <= qq).astype(np.float32)
    pp = np.arange(128, dtype=np.int64)
    ii, jj = np.meshgrid(np.arange(128), np.arange(128), indexing="ij")
    ltc = (jj >= ii).astype(np.float32)
    sbase = np.zeros((1, 128), dtype=np.float32)
    for tile in range(16):
        for e in range(8):
            sbase[0, tile * 8 + e] = (tile // 8) * (8 * 384) + e * 384
    iota = np.arange(128, dtype=np.float32).reshape(128, 1)
    in_maps = []
    for c in range(8):
        b, hg = c // 4, c % 4
        tok_off = 2048 * hg
        half = tok_off // 4096
        agidx = np.stack(
            [(half * 2048 + kt * 128 + pp) * 4096 + (tok_off % 4096)
             for kt in range(NKT)], axis=1).astype(np.int32)
        in_maps.append({
            "xT": xT[b],
            "x_own": np.ascontiguousarray(x[b, tok_off:tok_off + TOK]),
            "wqT": wq_s[hg], "wkT": wk_s[hg], "wvT": wv_s[hg],
            "woT": woT, "routerT": routerT, "w1P": w1P, "w2T": w2T,
            "cosT": cosT, "sinT": sinT, "identD": ident, "diagD": diag,
            "agidx": agidx, "ltcD": ltc, "sbaseD": sbase, "iotaD": iota,
        })
    return in_maps


_NC_CACHE = {}


def kernel(L0_attn_x, L0_attn_Wq, L0_attn_Wk, L0_attn_Wv, L0_attn_Wo,
           L0_ffn_router_w, L0_ffn_w1, L0_ffn_w2, _dbg=False):
    in_maps = _host_prepare(L0_attn_x, L0_attn_Wq, L0_attn_Wk, L0_attn_Wv,
                            L0_attn_Wo, L0_ffn_router_w, L0_ffn_w1, L0_ffn_w2)
    key = bool(_dbg)
    if key not in _NC_CACHE:
        _NC_CACHE[key] = build(dbg=_dbg)
    nc = _NC_CACHE[key]
    res = run_bass_kernel_spmd(nc, in_maps, core_ids=list(range(8)))
    out = np.empty((2, S, D), dtype=np.float32)
    for c in range(8):
        b, hg = c // 4, c % 4
        out[b, 2048 * hg:2048 * (hg + 1)] = res.results[c]["out"]
    if _dbg:
        return out, res
    return out
